# version 04
# baseline (speedup 1.0000x reference)
import numpy as np
from contextlib import ExitStack

import concourse.bass as bass
import concourse.mybir as mybir
from concourse.bass_utils import run_bass_kernel_spmd

F32 = mybir.dt.float32
BF16 = mybir.dt.bfloat16
I8 = mybir.dt.int8
AF = mybir.ActivationFunctionType
ALU = mybir.AluOpType

NCORE = 8
H = 16
C = 2048
KC = 16
T = 1024
NI = 64
RND = 4
IPR = 16
NCH = 8
HHALF = 8


class Eng:

    def __init__(self, name):
        self.name = name
        self.sem = None
        self.n = 0
        self.q = []

    def wait(self, other, val):
        if val is None or val <= 0:
            return
        self.q.append(lambda E, s=other.sem, v=val: E.wait_ge(s, v))

    def do(self, fn, inc=False, incby=1):
        if inc:
            self.n += incby
            self.q.append(lambda E, fn=fn, ib=incby, s=self: fn(E).then_inc(s.sem, ib))
            return self.n
        self.q.append(lambda E, fn=fn: fn(E))
        return None


def build_program(upto="all"):
    nc = bass.Bass("TRN2", target_bir_lowering=False, detect_race_conditions=False)

    def din(name, shape, dty):
        return nc.dram_tensor(name, list(shape), dty, kind="ExternalInput").ap()

    xin = din("xin", [C, T], F32)
    wqkvt = din("wqkvt", [48, 128, 2048], F32)
    wgt = din("wgt", [NI, 128, 2048], I8)
    wut = din("wut", [NI, 128, 2048], I8)
    wdt = din("wdt", [RND, KC, 128, 2048], I8)
    woutt = din("woutt", [2, KC, 128, 1024], F32)
    bqkv = din("bqkv", [6144], F32)
    sgate = din("sgate", [8192], F32)
    sup = din("sup", [8192], F32)
    sdown = din("sdown", [2048], F32)
    boutv = din("boutv", [2048], F32)
    slopes = din("slopes", [16], F32)
    ident8 = din("ident8", [128, 128], I8)
    identf = din("identf", [128, 128], F32)
    d0c = din("d0c", [128, 128], F32)
    m01c = din("m01c", [128, 128], F32)
    dqc = din("dqc", [128, 128], F32)
    dk16c = din("dk16c", [128, 16], F32)
    dbd8c = din("dbd8c", [128, 128], F32)
    dcrwc = din("dcrwc", [128, 128], F32)
    zcc = din("zcc", [128, 128], F32)

    xout = nc.dram_tensor("xout", [C, T], F32, kind="ExternalOutput").ap()
    agin = [nc.dram_tensor(f"agin{i}", [HHALF * 128, 128], F32) for i in range(2)]
    agout = [nc.dram_tensor(f"agout{i}", [NCORE * HHALF * 128, 128], F32) for i in range(2)]

    TEN, VEC, SCA, GPS, SYN = (Eng(n) for n in ("tensor", "vector", "scalar", "gpsimd", "sync"))
    ENGS = dict(tensor=TEN, vector=VEC, scalar=SCA, gpsimd=GPS, sync=SYN)

    with ExitStack() as top:
        for e in ENGS.values():
            e.sem = top.enter_context(nc.semaphore(f"sem_{e.name}"))
        sem_c = [top.enter_context(nc.semaphore(f"sem_c{i}")) for i in range(2)]

        sb = lambda name, shape, dty: top.enter_context(nc.sbuf_tensor(name, list(shape), dty))

        xT = sb("xT", [128, KC * T], F32)
        xnT = sb("xnT", [128, KC * T], BF16)
        wA = [sb(f"wA{i}", [128, 2048], BF16) for i in range(2)]
        wB = [sb(f"wB{i}", [128, 2048], BF16) for i in range(2)]
        wC = [sb(f"wC{i}", [128, 2048], BF16) for i in range(2)]
        identb_s = sb("identb_s", [128, 128], BF16)
        masks = sb("masks", [128, H * 128], BF16)
        qdec = sb("qdec", [128, H * 128], BF16)
        kdecs = sb("kdecs", [128, 16], F32)
        bds = sb("bds", [128, 16], F32)
        bdpow = sb("bdpow", [128, 128], F32)
        rw = sb("rw", [128, 128], F32)
        biasA = sb("biasA", [128, 128], F32)
        biasB = sb("biasB", [128, 80], F32)
        sdh = sb("sdh", [128, 16], F32)
        slope_bc = sb("slope_bc", [128, 16], F32)
        ones_b = sb("ones_b", [128, 1], BF16)
        ones_f = sb("ones_f", [1, 128], F32)
        eps_t = sb("eps_t", [128, 1], F32)
        rowA = sb("rowA", [1, T], F32)
        rowB = sb("rowB", [1, T], F32)
        rowC = sb("rowC", [1, T], F32)
        row_m, row_mn, row_msq = rowA, rowB, rowC
        row_var, row_std, row_rstd = rowA, rowC, rowA

        rel = {}

        def acq(eng, slot):
            r = rel.get(slot)
            if r is not None:
                eng.wait(r[0], r[1])

        def release(slot, eng, val):
            rel[slot] = (eng, val)

        def flush():
            for name, e in ENGS.items():
                fns = e.q
                e.q = []
                if fns:
                    getattr(block, name)(lambda E, fns=fns: [f(E) for f in fns])

        def gdma(out, in_, slot=None):
            if slot is not None:
                acq(GPS, slot)
            return GPS.do(lambda E, o=out, i=in_: E.dma_start(out=o, in_=i), inc=True, incby=16)

        def ydma(out, in_):
            return SYN.do(lambda E, o=out, i=in_: E.dma_start(out=o, in_=i), inc=True, incby=16)

        block = top.enter_context(nc.Block())

        with ExitStack() as sctx:
            ssb = lambda name, shape, dty: sctx.enter_context(nc.sbuf_tensor(name, list(shape), dty))
            identf_s = ssb("identf_s", [128, 128], F32)
            d0_s = ssb("d0_s", [128, 128], F32)
            m01_s = ssb("m01_s", [128, 128], F32)
            dq_s = ssb("dq_s", [128, 128], F32)
            dk16_s = ssb("dk16_s", [128, 16], F32)
            dbd8_s = ssb("dbd8_s", [128, 128], F32)
            dcrw_s = ssb("dcrw_s", [128, 128], F32)
            zc_s = ssb("zc_s", [128, 128], F32)
            slope_rh = ssb("slope_rh", [128, 128], F32)
            slope_h8 = ssb("slope_h8", [128, 128], F32)
            stage1 = ssb("stage1", [128, 128], F32)
            stage2 = ssb("stage2", [128, 128], F32)
            mexp = [ssb(f"mexp{i}", [128, 128], F32) for i in range(2)]
            pst1 = sctx.enter_context(nc.psum_tensor("pst1", [128, 512], F32))
            pst2 = sctx.enter_context(nc.psum_tensor("pst2", [128, 512], F32))

            v_xload = ydma(xT[:, :].rearrange("p (k t) -> p k t", k=KC),
                           xin.rearrange("(k p) t -> p k t", p=128))
            for dst, src in ((identf_s, identf), (d0_s, d0c), (m01_s, m01c), (dq_s, dqc),
                             (dk16_s, dk16c), (dbd8_s, dbd8c), (dcrw_s, dcrwc), (zc_s, zcc)):
                ydma(dst[:, :], src)
            ydma(slope_bc[:, :], slopes[None, :].broadcast_to([128, 16]))
            ydma(slope_rh[:, :].rearrange("p (r h) -> p r h", r=8),
                 slopes[None, None, :].broadcast_to([128, 8, 16]))
            ydma(slope_h8[:, :].rearrange("p (t h) -> p t h", t=8),
                 slopes[None, None, :].broadcast_to([128, 8, 16]))
            ydma(stage1[0:48, :], bqkv.rearrange("(a b) -> a b", b=128))
            ydma(stage1[48:112, :], sgate.rearrange("(a b) -> a b", b=128))
            ydma(stage1[112:128, :], sdown.rearrange("(a b) -> a b", b=128))
            ydma(stage2[0:64, :], sup.rearrange("(a b) -> a b", b=128))
            v_sload = ydma(stage2[64:80, :], boutv.rearrange("(a b) -> a b", b=128))
            gdma(identb_s[:, :], ident8)

            TEN.wait(SYN, v_sload)
            t_tr1 = TEN.do(lambda E: E.transpose(pst1[:, 0:128], stage1[:, :], identf_s[:, :]), inc=True)
            t_tr2 = TEN.do(lambda E: E.transpose(pst2[:, 0:128], stage2[:, :], identf_s[:, :]), inc=True)
            VEC.wait(TEN, t_tr1)
            VEC.do(lambda E: E.tensor_copy(biasA[:, :], pst1[:, 0:128]))
            VEC.wait(TEN, t_tr2)
            v_bB = VEC.do(lambda E: E.tensor_copy(biasB[:, :], pst2[:, 0:80]), inc=True)
            VEC.wait(VEC, v_bB)
            VEC.do(lambda E: E.tensor_scalar_mul(sdh[:, :], biasA[:, 112:128], 0.5))
            VEC.do(lambda E: E.memset(ones_b[:, :], 1.0))
            VEC.do(lambda E: E.memset(ones_f[:, :], 1.0))
            VEC.do(lambda E: E.memset(eps_t[:, :], 1e-5))

            SCA.wait(SYN, v_sload)
            s_bds = SCA.do(lambda E: E.activation(bds[:, :], slope_bc[:, :], AF.Exp, scale=-128.0), inc=True)
            VEC.do(lambda E: E.tensor_mul(mexp[0][:, 0:16], dk16_s[:, :], slope_bc[:, :]))
            v_kd = VEC.do(lambda E: E.tensor_mul(mexp[1][:, :], dbd8_s[:, :], slope_h8[:, :]), inc=True)
            SCA.wait(VEC, v_kd)
            SCA.do(lambda E: E.activation(kdecs[:, :], mexp[0][:, 0:16], AF.Exp))
            s_bdp = SCA.do(lambda E: E.activation(bdpow[:, :], mexp[1][:, :], AF.Exp), inc=True)
            VEC.wait(SCA, s_bdp)
            v_rw1 = VEC.do(lambda E: E.tensor_mul(mexp[0][:, :], dcrw_s[:, :], slope_rh[:, :]), inc=True)
            SCA.wait(VEC, v_rw1)
            SCA.wait(SCA, s_bdp)
            s_rw = SCA.do(lambda E: E.activation(mexp[1][:, :], mexp[0][:, :], AF.Exp), inc=True)
            VEC.wait(SCA, s_rw)
            VEC.do(lambda E: E.tensor_mul(rw[:, :], mexp[1][:, :], zc_s[:, :]))
            for h in range(H):
                sc = slope_bc[:, h:h + 1]
                s_m = SCA.do(lambda E, sc=sc, h=h: E.activation(
                    mexp[h % 2][:, :], d0_s[:, :], AF.Exp, scale=sc), inc=True)
                VEC.wait(SCA, s_m)
                v_m = VEC.do(lambda E, h=h: E.tensor_mul(
                    masks[:, h * 128:(h + 1) * 128], mexp[h % 2][:, :], m01_s[:, :]), inc=True)
                SCA.wait(VEC, v_m)
                SCA.do(lambda E, sc=sc, h=h: E.activation(
                    qdec[:, h * 128:(h + 1) * 128], dq_s[:, :], AF.Exp, scale=sc))
            flush()

        def layer_norm(tag, xready_eng, xready_val, sub="all"):
            with ExitStack() as ctx:
                ps1 = ctx.enter_context(nc.psum_tensor(f"ps1{tag}", [128, T], F32))
                ps2 = ctx.enter_context(nc.psum_tensor(f"ps2{tag}", [128, T], F32))
                psa = ctx.enter_context(nc.psum_tensor(f"psa{tag}", [128, T], F32))
                psb = ctx.enter_context(nc.psum_tensor(f"psb{tag}", [128, T], F32))
                xb2 = [ctx.enter_context(nc.sbuf_tensor(f"xb{tag}{i}", [128, T], BF16)) for i in range(2)]
                sq2 = [ctx.enter_context(nc.sbuf_tensor(f"sq{tag}{i}", [128, T], BF16)) for i in range(2)]
                mm_done = [0] * KC
                for k in range(KC):
                    if k == 0:
                        SCA.wait(xready_eng, xready_val)
                    acq(SCA, f"xb{k % 2}")
                    xb_done = SCA.do(lambda E, k=k: E.activation(
                        xb2[k % 2][:, :], xT[:, k * T:(k + 1) * T], AF.Copy), inc=True)
                    VEC.wait(SCA, xb_done)
                    acq(VEC, f"sq{k % 2}")
                    sq_done = VEC.do(lambda E, k=k: E.tensor_mul(
                        sq2[k % 2][:, :], xb2[k % 2][:, :], xb2[k % 2][:, :]), inc=True)
                    TEN.wait(SCA, xb_done)
                    TEN.wait(VEC, sq_done)
                    if k == 0:
                        acq(TEN, "ln_psum")
                    st, sp = (k == 0), (k == KC - 1)
                    for half in range(2):
                        TEN.do(lambda E, k=k, half=half, st=st, sp=sp: E.matmul(
                            ps1[0:1, half * 512:(half + 1) * 512], ones_b[:, :],
                            xb2[k % 2][:, half * 512:(half + 1) * 512], start=st, stop=sp))
                    TEN.do(lambda E, k=k, st=st, sp=sp: E.matmul(
                        ps2[0:1, 0:512], ones_b[:, :], sq2[k % 2][:, 0:512], start=st, stop=sp))
                    mm_done[k] = TEN.do(lambda E, k=k, st=st, sp=sp: E.matmul(
                        ps2[0:1, 512:1024], ones_b[:, :], sq2[k % 2][:, 512:1024],
                        start=st, stop=sp), inc=True)
                    release(f"xb{k % 2}", TEN, mm_done[k])
                    release(f"sq{k % 2}", TEN, mm_done[k])
                if sub == "stats":
                    flush()
                    return ("TEN", mm_done[KC - 1])
                VEC.wait(TEN, mm_done[KC - 1])
                v_r1 = VEC.do(lambda E: E.tensor_scalar_mul(row_m[:, :], ps1[0:1, :], 1.0 / C), inc=True)
                VEC.wait(VEC, v_r1)
                VEC.do(lambda E: E.tensor_scalar_mul(row_mn[:, :], row_m[:, :], -1.0))
                v_r2 = VEC.do(lambda E: E.tensor_mul(row_msq[:, :], row_m[:, :], row_m[:, :]), inc=True)
                VEC.wait(VEC, v_r2)
                v_r3 = VEC.do(lambda E: E.tensor_scalar_mul(row_msq[:, :], row_msq[:, :], -1.0), inc=True)
                VEC.wait(VEC, v_r3)
                v_var = VEC.do(lambda E: E.scalar_tensor_tensor(
                    out=row_var[:, :], in0=ps2[0:1, :], scalar=1.0 / C, in1=row_msq[:, :],
                    op0=ALU.mult, op1=ALU.add), inc=True)
                SCA.wait(VEC, v_var)
                s_std = SCA.do(lambda E: E.activation(row_std[:, :], row_var[:, :], AF.Sqrt, bias=eps_t[0:1, 0:1]), inc=True)
                VEC.wait(SCA, s_std)
                v_rstd = VEC.do(lambda E: E.reciprocal(row_rstd[:, :], row_std[:, :]), inc=True)
                if sub == "rows":
                    flush()
                    return ("VEC", v_rstd)
                TEN.wait(VEC, v_rstd)
                for half in range(2):
                    TEN.do(lambda E, half=half: E.matmul(
                        psa[:, half * 512:(half + 1) * 512], ones_f[:, :],
                        row_rstd[0:1, half * 512:(half + 1) * 512], start=True, stop=True))
                TEN.do(lambda E: E.matmul(
                    psb[:, 0:512], ones_f[:, :], row_mn[0:1, 0:512], start=True, stop=True))
                t_bc = TEN.do(lambda E: E.matmul(
                    psb[:, 512:1024], ones_f[:, :], row_mn[0:1, 512:1024],
                    start=True, stop=True), inc=True)
                if sub == "bcast":
                    flush()
                    return ("TEN", t_bc)
                VEC.wait(TEN, t_bc)
                v_xn = 0
                for k in range(KC):
                    v_n1 = VEC.do(lambda E, k=k: E.tensor_tensor(
                        xnT[:, k * T:(k + 1) * T], xT[:, k * T:(k + 1) * T], psb[:, :], ALU.add),
                        inc=True)
                    VEC.wait(VEC, v_n1)
                    v_xn = VEC.do(lambda E, k=k: E.tensor_tensor(
                        xnT[:, k * T:(k + 1) * T], xnT[:, k * T:(k + 1) * T], psa[:, :], ALU.mult),
                        inc=True)
                release("ln_psum", VEC, v_xn)
                flush()
                return v_xn

        def ffn_pass(pass_idx, v_xn):
            with ExitStack() as ctx:
                fuse = ctx.enter_context(nc.sbuf_tensor(f"fuse{pass_idx}", [128, IPR * T], BF16))
                gtmp = [ctx.enter_context(nc.sbuf_tensor(f"gt{pass_idx}_{i}", [128, T], BF16)) for i in range(2)]
                utmp = [ctx.enter_context(nc.sbuf_tensor(f"ut{pass_idx}_{i}", [128, T], BF16)) for i in range(2)]
                pg = [ctx.enter_context(nc.psum_tensor(f"pg{pass_idx}_{i}", [128, T], F32)) for i in range(2)]
                pu = [ctx.enter_context(nc.psum_tensor(f"pu{pass_idx}_{i}", [128, T], F32)) for i in range(2)]
                psy = pg + pu
                pskey = ["pg0", "pg1", "pu0", "pu1"]
                for r in range(RND):
                    v_fuse_round = 0
                    for ii in range(IPR):
                        gi = r * IPR + ii
                        nwg = gdma(wA[gi % 2][:, :], wgt[gi], slot=f"wA{gi % 2}")
                        nwu = gdma(wB[gi % 2][:, :], wut[gi], slot=f"wB{gi % 2}")
                        if gi == 0:
                            TEN.wait(VEC, v_xn)
                        TEN.wait(GPS, nwg)
                        acq(TEN, f"pg{gi % 2}")
                        t_g = 0
                        for half in range(2):
                            for k in range(KC):
                                last = (half == 1 and k == KC - 1)
                                t_g = TEN.do(lambda E, gi=gi, half=half, k=k: E.matmul(
                                    pg[gi % 2][:, half * 512:(half + 1) * 512],
                                    wA[gi % 2][:, k * 128:(k + 1) * 128],
                                    xnT[:, k * T + half * 512: k * T + (half + 1) * 512],
                                    start=(k == 0), stop=(k == KC - 1)), inc=last)
                        release(f"wA{gi % 2}", TEN, t_g)
                        TEN.wait(GPS, nwu)
                        acq(TEN, f"pu{gi % 2}")
                        t_u = 0
                        for half in range(2):
                            for k in range(KC):
                                last = (half == 1 and k == KC - 1)
                                t_u = TEN.do(lambda E, gi=gi, half=half, k=k: E.matmul(
                                    pu[gi % 2][:, half * 512:(half + 1) * 512],
                                    wB[gi % 2][:, k * 128:(k + 1) * 128],
                                    xnT[:, k * T + half * 512: k * T + (half + 1) * 512],
                                    start=(k == 0), stop=(k == KC - 1)), inc=last)
                        release(f"wB{gi % 2}", TEN, t_u)
                        SCA.wait(TEN, t_g)
                        acq(SCA, f"gtmp{gi % 2}")
                        SCA.do(lambda E, gi=gi: E.activation(
                            gtmp[gi % 2][:, :], pg[gi % 2][:, :], AF.Silu,
                            scale=biasA[:, 48 + gi % 64:49 + gi % 64]))
                        SCA.wait(TEN, t_u)
                        acq(SCA, f"utmp{gi % 2}")
                        s_e = SCA.do(lambda E, gi=gi: E.activation(
                            utmp[gi % 2][:, :], pu[gi % 2][:, :], AF.Copy,
                            scale=biasB[:, gi % 64:gi % 64 + 1]), inc=True)
                        release(f"pg{gi % 2}", SCA, s_e)
                        release(f"pu{gi % 2}", SCA, s_e)
                        VEC.wait(SCA, s_e)
                        acq(VEC, f"fuse{ii}")
                        v_f = VEC.do(lambda E, gi=gi, ii=ii: E.tensor_mul(
                            fuse[:, ii * T:(ii + 1) * T], gtmp[gi % 2][:, :], utmp[gi % 2][:, :]), inc=True)
                        release(f"gtmp{gi % 2}", VEC, v_f)
                        release(f"utmp{gi % 2}", VEC, v_f)
                        if ii == IPR - 1:
                            v_fuse_round = v_f
                    t_y_last = 0
                    for m in range(KC):
                        nwd = gdma(wC[m % 2][:, :], wdt[r, m], slot=f"wC{m % 2}")
                        TEN.wait(VEC, v_fuse_round)
                        TEN.wait(GPS, nwd)
                        acq(TEN, pskey[m % 4])
                        t_y = 0
                        for half in range(2):
                            for ii in range(IPR):
                                last = (half == 1 and ii == IPR - 1)
                                t_y = TEN.do(lambda E, m=m, half=half, ii=ii: E.matmul(
                                    psy[m % 4][:, half * 512:(half + 1) * 512],
                                    wC[m % 2][:, ii * 128:(ii + 1) * 128],
                                    fuse[:, ii * T + half * 512: ii * T + (half + 1) * 512],
                                    start=(ii == 0), stop=(ii == IPR - 1)), inc=last)
                        release(f"wC{m % 2}", TEN, t_y)
                        t_y_last = t_y
                        VEC.wait(TEN, t_y)
                        v_mg = VEC.do(lambda E, m=m: E.scalar_tensor_tensor(
                            out=xT[:, m * T:(m + 1) * T], in0=psy[m % 4][:, :],
                            scalar=sdh[:, m:m + 1], in1=xT[:, m * T:(m + 1) * T],
                            op0=ALU.mult, op1=ALU.add), inc=True)
                        release(pskey[m % 4], VEC, v_mg)
                        if pass_idx == 1 and r == RND - 1:
                            SYN.wait(VEC, v_mg)
                            ydma(xout[m * 128:(m + 1) * 128, :], xT[:, m * T:(m + 1) * T])
                    for ii in range(IPR):
                        release(f"fuse{ii}", TEN, t_y_last)
                v_last_merge = VEC.n
                flush()
                return v_last_merge

        def attn_half(half_idx, v_xn, o_sb, qhat, hb, sbufs, prev_done):
            qTh, kTh, vT_scr, vtok = hb
            scb, qt_scr, kt_scr, Sb, Sf = sbufs
            h0 = half_idx * HHALF
            with ExitStack() as ctx:
                pJ0 = ctx.enter_context(nc.psum_tensor(f"pJ0{half_idx}", [128, T], F32))
                pJ1 = ctx.enter_context(nc.psum_tensor(f"pJ1{half_idx}", [128, T], F32))
                pQ, pK, pV = pJ0, pJ1, pJ0
                pscS_t = ctx.enter_context(nc.psum_tensor(f"pscS{half_idx}", [128, 512], F32))
                po_t = ctx.enter_context(nc.psum_tensor(f"po{half_idx}", [128, 512], F32))
                psc, pS, po = pscS_t[:, 0:128], pscS_t[:, 256:384], po_t[:, 0:128]
                pbkA = ctx.enter_context(nc.psum_tensor(f"pbkA{half_idx}", [128, 1024], BF16))
                pbkB = ctx.enter_context(nc.psum_tensor(f"pbkB{half_idx}", [128, 1024], BF16))
                pktr = pbkA[:, 0:128]
                ptrv = [pbkA[:, 128:256], pbkB[:, 0:128]]

                n_ast = 0
                for hh in range(HHALF):
                    h = h0 + hh
                    nq = gdma(wB[hh % 2][:, :], wqkvt[h], slot=f"wB{hh % 2}")
                    nk = gdma(wC[hh % 2][:, :], wqkvt[16 + h], slot=f"wC{hh % 2}")
                    nv = gdma(wA[hh % 2][:, :], wqkvt[32 + h], slot=f"wA{hh % 2}")
                    if hh == 0:
                        TEN.wait(VEC, v_xn)
                        if prev_done:
                            TEN.wait(VEC, prev_done)
                    TEN.wait(GPS, nq)
                    acq(TEN, "pQ")
                    t_q = 0
                    for half in range(2):
                        for k in range(KC):
                            last = (half == 1 and k == KC - 1)
                            t_q = TEN.do(lambda E, hh=hh, half=half, k=k: E.matmul(
                                pQ[:, half * 512:(half + 1) * 512],
                                wB[hh % 2][:, k * 128:(k + 1) * 128],
                                xnT[:, k * T + half * 512: k * T + (half + 1) * 512],
                                start=(k == 0), stop=(k == KC - 1)), inc=last)
                    release(f"wB{hh % 2}", TEN, t_q)
                    TEN.wait(GPS, nk)
                    acq(TEN, "pK")
                    t_k = 0
                    for half in range(2):
                        for k in range(KC):
                            last = (half == 1 and k == KC - 1)
                            t_k = TEN.do(lambda E, hh=hh, half=half, k=k: E.matmul(
                                pK[:, half * 512:(half + 1) * 512],
                                wC[hh % 2][:, k * 128:(k + 1) * 128],
                                xnT[:, k * T + half * 512: k * T + (half + 1) * 512],
                                start=(k == 0), stop=(k == KC - 1)), inc=last)
                    release(f"wC{hh % 2}", TEN, t_k)
                    SCA.wait(TEN, t_q)
                    acq(SCA, "qTh")
                    s_evq = SCA.do(lambda E, h=h: E.activation(
                        qTh[:, :], pQ[:, :], AF.Identity, bias=biasA[:, h:h + 1]), inc=True)
                    SCA.wait(TEN, t_k)
                    acq(SCA, "kTh")
                    s_evk = SCA.do(lambda E, h=h: E.activation(
                        kTh[:, :], pK[:, :], AF.Identity, bias=biasA[:, 16 + h:17 + h]), inc=True)
                    release("pK", SCA, s_evk)
                    TEN.wait(GPS, nv)
                    TEN.wait(SCA, s_evq)
                    t_v = 0
                    for half in range(2):
                        for k in range(KC):
                            last = (half == 1 and k == KC - 1)
                            t_v = TEN.do(lambda E, hh=hh, half=half, k=k: E.matmul(
                                pV[:, half * 512:(half + 1) * 512],
                                wA[hh % 2][:, k * 128:(k + 1) * 128],
                                xnT[:, k * T + half * 512: k * T + (half + 1) * 512],
                                start=(k == 0), stop=(k == KC - 1)), inc=last)
                    release(f"wA{hh % 2}", TEN, t_v)
                    SCA.wait(TEN, t_v)
                    acq(SCA, "vT_scr")
                    s_qkv = SCA.do(lambda E, h=h: E.activation(
                        vT_scr[:, :], pV[:, :], AF.Identity, bias=biasA[:, 32 + h:33 + h]), inc=True)
                    release("pQ", SCA, s_qkv)
                    TEN.wait(SCA, s_qkv)
                    v_vtok = 0
                    t_tr = 0
                    for t in range(NCH):
                        acq(TEN, f"ptrv{t % 2}")
                        t_tr = TEN.do(lambda E, t=t: E.transpose(
                            ptrv[t % 2], vT_scr[:, t * 128:(t + 1) * 128], identb_s[:, :]), inc=True)
                        VEC.wait(TEN, t_tr)
                        acq(VEC, f"vtok{t}")
                        v_vtok = VEC.do(lambda E, t=t: E.tensor_copy(
                            vtok[:, t * 128:(t + 1) * 128], ptrv[t % 2]), inc=True)
                        release(f"ptrv{t % 2}", VEC, v_vtok)
                    release("vT_scr", TEN, t_tr)
                    acq(VEC, f"Sf{hh % 2}")
                    acq(VEC, f"Sfv{hh % 2}")
                    VEC.do(lambda E, hh=hh: E.memset(Sf[hh % 2][:, :], 0.0))
                    v_su = 0
                    v_qt = 0
                    t_kt = 0
                    for t in range(NCH):
                        if t == 0:
                            TEN.wait(VEC, v_vtok)
                        acq(TEN, "psc")
                        t_sc = TEN.do(lambda E, t=t: E.matmul(
                            psc, kTh[:, t * 128:(t + 1) * 128], qTh[:, t * 128:(t + 1) * 128],
                            start=True, stop=True), inc=True)
                        VEC.wait(TEN, t_sc)
                        acq(VEC, "scb")
                        v_sc = VEC.do(lambda E, h=h: E.tensor_mul(
                            scb[:, :], psc, masks[:, h * 128:(h + 1) * 128]), inc=True)
                        acq(VEC, "qt_scr")
                        v_q0 = VEC.do(lambda E, h=h, t=t: E.tensor_mul(
                            qt_scr[:, :], qTh[:, t * 128:(t + 1) * 128], qdec[:, h * 128:(h + 1) * 128]),
                            inc=True)
                        VEC.wait(VEC, v_q0)
                        v_qt = VEC.do(lambda E, h=h, hh=hh, t=t: E.tensor_scalar_mul(
                            qhat[:, hh * T + t * 128: hh * T + (t + 1) * 128], qt_scr[:, :],
                            bdpow[:, t * 16 + h: t * 16 + h + 1]), inc=True)
                        TEN.wait(VEC, v_qt)
                        acq(TEN, "po")
                        if t == 0:
                            t_o = TEN.do(lambda E, t=t: E.matmul(
                                po, vtok[:, t * 128:(t + 1) * 128], scb[:, :],
                                start=True, stop=True), inc=True)
                        else:
                            TEN.do(lambda E, t=t: E.matmul(
                                po, vtok[:, t * 128:(t + 1) * 128], scb[:, :],
                                start=True, stop=False))
                            t_o = TEN.do(lambda E: E.matmul(
                                po, Sb[:, :], qt_scr[:, :], start=False, stop=True), inc=True)
                        release("scb", TEN, t_o)
                        release("qt_scr", TEN, t_o)
                        acq(TEN, "pktr")
                        t_kt = TEN.do(lambda E, t=t: E.transpose(
                            pktr, kTh[:, t * 128:(t + 1) * 128], identb_s[:, :]), inc=True)
                        SCA.wait(TEN, t_kt)
                        acq(SCA, "kt_scr")
                        s_kt = SCA.do(lambda E, h=h: E.activation(
                            kt_scr[:, :], pktr, AF.Copy, scale=kdecs[:, h:h + 1]), inc=True)
                        release("pktr", SCA, s_kt)
                        TEN.wait(SCA, s_kt)
                        acq(TEN, "pS")
                        t_S = TEN.do(lambda E, t=t: E.matmul(
                            pS, kt_scr[:, :], vtok[:, t * 128:(t + 1) * 128],
                            start=True, stop=True), inc=True)
                        release("kt_scr", TEN, t_S)
                        VEC.wait(TEN, t_S)
                        v_su = VEC.do(lambda E, h=h, hh=hh: E.scalar_tensor_tensor(
                            out=Sf[hh % 2][:, :], in0=Sf[hh % 2][:, :], scalar=bds[:, h:h + 1],
                            in1=pS, op0=ALU.mult, op1=ALU.add), inc=True)
                        release("pS", VEC, v_su)
                        release("psc", VEC, v_su)
                        if t < NCH - 1:
                            VEC.wait(TEN, t_o)
                            VEC.wait(VEC, v_su)
                            VEC.do(lambda E, hh=hh: E.tensor_copy(Sb[:, :], Sf[hh % 2][:, :]))
                        SCA.wait(TEN, t_o)
                        s_oev = SCA.do(lambda E, hh=hh, t=t: E.activation(
                            o_sb[:, hh * T + t * 128: hh * T + (t + 1) * 128], po, AF.Copy), inc=True)
                        release("po", SCA, s_oev)
                    SYN.wait(VEC, v_su)
                    n_ast = ydma(agin[half_idx].ap()[hh * 128:(hh + 1) * 128, :], Sf[hh % 2][:, :])
                    release(f"Sf{hh % 2}", SYN, n_ast)
                    release(f"Sfv{hh % 2}", VEC, v_su)
                    release("qTh", VEC, v_qt)
                    release("kTh", TEN, t_kt)
                    for t in range(NCH):
                        release(f"vtok{t}", TEN, t_S)
                GPS.wait(SYN, n_ast)
                GPS.do(lambda E, hi=half_idx, cs=sem_c[half_idx]: E.collective_compute(
                    "AllGather", ALU.bypass, replica_groups=[list(range(NCORE))],
                    ins=[agin[hi].ap().opt()], outs=[agout[hi].ap().opt()]).then_inc(cs, 1))
                scan_end = dict(t=TEN.n, v=VEC.n, s=SCA.n)
                flush()
                return scan_end

        def attn_global(half_idx, o_sb, qhat, scan_end, agl2, S_in, S_in_b):
            h0 = half_idx * HHALF
            with ExitStack() as ctx:
                pog = [ctx.enter_context(nc.psum_tensor(f"pog{half_idx}_{i}", [128, T], F32)) for i in range(2)]
                pgo = [ctx.enter_context(nc.psum_tensor(f"pgo{half_idx}_{i}", [128, T], F32)) for i in range(2)]
                SYN.q.append(lambda E, cs=sem_c[half_idx]: E.wait_ge(cs, 1))
                SYN.wait(TEN, scan_end["t"])
                SYN.wait(VEC, scan_end["v"])
                SYN.wait(SCA, scan_end["s"])
                v_cmb = 0
                for r in range(NCORE):
                    acq(SYN, f"agl{r % 2}")
                    n_l = ydma(agl2[r % 2][:, :].rearrange("d (h e) -> d h e", h=HHALF),
                               agout[half_idx].ap()[r * HHALF * 128:(r + 1) * HHALF * 128, :]
                               .rearrange("(h d) e -> d h e", h=HHALF))
                    VEC.wait(SYN, n_l)
                    for hh in range(HHALF):
                        h = h0 + hh
                        if r == 0:
                            if hh == 0:
                                acq(VEC, "S_in_cast")
                            v_cmb = VEC.do(lambda E, r=r, hh=hh, h=h: E.tensor_scalar_mul(
                                S_in[:, hh * 128:(hh + 1) * 128],
                                agl2[r % 2][:, hh * 128:(hh + 1) * 128],
                                rw[:, r * 16 + h: r * 16 + h + 1]), inc=(hh == HHALF - 1))
                        else:
                            v_cmb = VEC.do(lambda E, r=r, hh=hh, h=h: E.scalar_tensor_tensor(
                                out=S_in[:, hh * 128:(hh + 1) * 128],
                                in0=agl2[r % 2][:, hh * 128:(hh + 1) * 128],
                                scalar=rw[:, r * 16 + h: r * 16 + h + 1],
                                in1=S_in[:, hh * 128:(hh + 1) * 128],
                                op0=ALU.mult, op1=ALU.add), inc=(hh == HHALF - 1))
                    release(f"agl{r % 2}", VEC, v_cmb)
                SCA.wait(VEC, v_cmb)
                acq(SCA, "S_in_b")
                s_sib = SCA.do(lambda E: E.activation(S_in_b[:, :], S_in[:, :], AF.Copy), inc=True)
                release("S_in_cast", SCA, s_sib)
                t_og = 0
                v_oadd = 0
                for hh in range(HHALF):
                    if hh == 0:
                        TEN.wait(SCA, s_sib)
                        VEC.wait(SCA, scan_end["s"])
                    acq(TEN, f"pog{hh % 2}")
                    for half in range(2):
                        t_og = TEN.do(lambda E, hh=hh, half=half: E.matmul(
                            pog[hh % 2][:, half * 512:(half + 1) * 512],
                            S_in_b[:, hh * 128:(hh + 1) * 128],
                            qhat[:, hh * T + half * 512: hh * T + (half + 1) * 512],
                            start=True, stop=True), inc=(half == 1))
                    VEC.wait(TEN, t_og)
                    v_oadd = VEC.do(lambda E, hh=hh: E.tensor_tensor(
                        o_sb[:, hh * T:(hh + 1) * T], o_sb[:, hh * T:(hh + 1) * T],
                        pog[hh % 2][:, :], ALU.add), inc=True)
                    release(f"pog{hh % 2}", VEC, v_oadd)
                release("S_in_b", TEN, t_og)
                v_m2 = 0
                for m in range(KC):
                    nw = gdma(wA[m % 2][:, 0:1024], woutt[half_idx, m], slot=f"wA{m % 2}")
                    if m == 0:
                        TEN.wait(VEC, v_oadd)
                    TEN.wait(GPS, nw)
                    acq(TEN, f"pgo{m % 2}")
                    t_po = 0
                    for half in range(2):
                        for hh in range(HHALF):
                            last = (half == 1 and hh == HHALF - 1)
                            t_po = TEN.do(lambda E, m=m, half=half, hh=hh: E.matmul(
                                pgo[m % 2][:, half * 512:(half + 1) * 512],
                                wA[m % 2][:, hh * 128:(hh + 1) * 128],
                                o_sb[:, hh * T + half * 512: hh * T + (half + 1) * 512],
                                start=(hh == 0), stop=(hh == HHALF - 1)), inc=last)
                    release(f"wA{m % 2}", TEN, t_po)
                    VEC.wait(TEN, t_po)
                    if half_idx == 0:
                        v_m2 = VEC.do(lambda E, m=m: E.scalar_tensor_tensor(
                            out=xT[:, m * T:(m + 1) * T], in0=pgo[m % 2][:, :],
                            scalar=biasB[:, 64 + m:65 + m], in1=xT[:, m * T:(m + 1) * T],
                            op0=ALU.add, op1=ALU.add), inc=True)
                    else:
                        v_m2 = VEC.do(lambda E, m=m: E.tensor_tensor(
                            xT[:, m * T:(m + 1) * T], xT[:, m * T:(m + 1) * T],
                            pgo[m % 2][:, :], ALU.add), inc=True)
                    release(f"pgo{m % 2}", VEC, v_m2)
                flush()
                return v_m2

        def _bail(ready_eng, ready_val):
            SYN.wait(ready_eng, ready_val)
            for m in range(KC):
                ydma(xout[m * 128:(m + 1) * 128, :], xT[:, m * T:(m + 1) * T])
            SYN.q.append(lambda E, s=SYN, v=SYN.n: E.wait_ge(s.sem, v))
            flush()

        if upto == "setup":
            _bail(SYN, v_xload)
            return nc
        v_xn1 = layer_norm("a", SYN, v_xload)
        if upto == "ln1":
            _bail(VEC, v_xn1)
            return nc
        v_ffn1 = ffn_pass(0, v_xn1)
        if upto == "ffn1":
            _bail(VEC, v_ffn1)
            return nc
        v_xn2 = layer_norm("b", VEC, v_ffn1)
        if upto == "ln2":
            _bail(VEC, v_xn2)
            return nc

        with ExitStack() as actx:
            asb = lambda name, shape, dty: actx.enter_context(nc.sbuf_tensor(name, list(shape), dty))
            qTh = asb("qTh", [128, T], BF16)
            kTh = asb("kTh", [128, T], BF16)
            vT_scr = asb("vT_scr", [128, T], BF16)
            vtok = asb("vtok", [128, T], BF16)
            scb = asb("scb", [128, 128], BF16)
            qt_scr = asb("qt_scr", [128, 128], BF16)
            kt_scr = asb("kt_scr", [128, 128], BF16)
            Sb = asb("Sb", [128, 128], BF16)
            Sf = [asb(f"Sf{i}", [128, 128], F32) for i in range(2)]
            agl2 = [asb(f"agl{i}", [128, HHALF * 128], F32) for i in range(2)]
            S_in = asb("S_in", [128, HHALF * 128], F32)
            S_in_b = asb("S_in_b", [128, HHALF * 128], BF16)
            hb = (qTh, kTh, vT_scr, vtok)
            sbufs = (scb, qt_scr, kt_scr, Sb, Sf)

            with ExitStack() as h1ctx:
                o1 = h1ctx.enter_context(nc.sbuf_tensor("o1", [128, HHALF * T], BF16))
                qh1 = h1ctx.enter_context(nc.sbuf_tensor("qh1", [128, HHALF * T], BF16))
                se1 = attn_half(0, v_xn2, o1, qh1, hb, sbufs, 0)
                if upto == "attn1":
                    _bail(SCA, se1["s"])
                    return nc
                v_g1 = attn_global(0, o1, qh1, se1, agl2, S_in, S_in_b)
                if upto == "attng1":
                    _bail(VEC, v_g1)
                    return nc
            with ExitStack() as h2ctx:
                o2 = h2ctx.enter_context(nc.sbuf_tensor("o2", [128, HHALF * T], BF16))
                qh2 = h2ctx.enter_context(nc.sbuf_tensor("qh2", [128, HHALF * T], BF16))
                se2 = attn_half(1, v_xn2, o2, qh2, hb, sbufs, v_g1)
                if upto == "attn2":
                    _bail(SCA, se2["s"])
                    return nc
                v_g2 = attn_global(1, o2, qh2, se2, agl2, S_in, S_in_b)
                if upto == "attng2":
                    _bail(VEC, v_g2)
                    return nc

        if upto.startswith("ln3:"):
            sub = upto.split(":")[1]
            rv = layer_norm("c", VEC, v_g2, sub=sub)
            eng = {"TEN": TEN, "VEC": VEC}[rv[0]]
            _bail(eng, rv[1])
            return nc
        v_xn3 = layer_norm("c", VEC, v_g2)
        if upto == "ln3":
            _bail(VEC, v_xn3)
            return nc
        ffn_pass(1, v_xn3)

        SYN.q.append(lambda E, s=SYN, v=SYN.n: E.wait_ge(s.sem, v))
        flush()

    return nc


_CACHE = {}


def _host_prep(L0_attn_x, L0_attn_Wqkv, L0_attn_bqkv, L0_attn_Wout, L0_attn_bout,
               L0_attn_slopes, L0_ffn_w_gate, L0_ffn_w_up, L0_ffn_w_down,
               L0_ffn_s_gate, L0_ffn_s_up, L0_ffn_s_down):
    f32 = np.float32
    x = np.asarray(L0_attn_x, f32)
    Wqkv = np.asarray(L0_attn_Wqkv, f32)
    Wout = np.asarray(L0_attn_Wout, f32)
    wg = np.asarray(L0_ffn_w_gate, np.int8)
    wu = np.asarray(L0_ffn_w_up, np.int8)
    wd = np.asarray(L0_ffn_w_down, np.int8)

    wqkvt = np.ascontiguousarray(
        Wqkv.reshape(16, 128, 48, 128).transpose(2, 1, 0, 3).reshape(48, 128, 2048))
    wgt = np.ascontiguousarray(
        wg.reshape(16, 128, 64, 128).transpose(2, 1, 0, 3).reshape(64, 128, 2048))
    wut = np.ascontiguousarray(
        wu.reshape(16, 128, 64, 128).transpose(2, 1, 0, 3).reshape(64, 128, 2048))
    wdt = np.ascontiguousarray(
        wd.reshape(4, 16, 128, 16, 128).transpose(0, 3, 2, 1, 4).reshape(4, 16, 128, 2048))
    woutt = np.ascontiguousarray(
        Wout.reshape(2, 8, 128, 16, 128).transpose(0, 3, 2, 1, 4).reshape(2, 16, 128, 1024))

    jj, ii = np.mgrid[0:128, 0:128]
    d0c = (-np.maximum(ii - jj, 0)).astype(f32)
    m01c = (ii >= jj).astype(f32)
    dqc = np.broadcast_to(-np.arange(128, dtype=f32)[None, :], (128, 128)).copy()
    dk16c = np.broadcast_to(-(128.0 - np.arange(128, dtype=f32))[:, None], (128, 16)).copy()
    dbd8c = np.broadcast_to(np.repeat(-128.0 * np.arange(8, dtype=f32), 16)[None, :], (128, 128)).copy()

    shared = dict(
        wqkvt=wqkvt, wgt=wgt, wut=wut, wdt=wdt, woutt=woutt,
        bqkv=np.asarray(L0_attn_bqkv, f32), sgate=np.asarray(L0_ffn_s_gate, f32),
        sup=np.asarray(L0_ffn_s_up, f32), sdown=np.asarray(L0_ffn_s_down, f32),
        boutv=np.asarray(L0_attn_bout, f32), slopes=np.asarray(L0_attn_slopes, f32),
        ident8=np.eye(128, dtype=np.int8), identf=np.eye(128, dtype=f32),
        d0c=d0c, m01c=m01c, dqc=dqc, dk16c=dk16c, dbd8c=dbd8c,
    )
    in_maps = []
    for c in range(NCORE):
        row = np.zeros(128, dtype=f32)
        zrow = np.zeros(128, dtype=f32)
        for r in range(8):
            if r < c:
                row[r * 16:(r + 1) * 16] = -1024.0 * (c - 1 - r)
                zrow[r * 16:(r + 1) * 16] = 1.0
        m = dict(shared)
        m["xin"] = np.ascontiguousarray(x[0, c * 1024:(c + 1) * 1024, :].T)
        m["dcrwc"] = np.broadcast_to(row[None, :], (128, 128)).copy()
        m["zcc"] = np.broadcast_to(zrow[None, :], (128, 128)).copy()
        in_maps.append(m)
    return in_maps


def kernel(**inputs):
    in_maps = _host_prep(**inputs)
    if "nc" not in _CACHE:
        _CACHE["nc"] = build_program()
    nc = _CACHE["nc"]
    res = run_bass_kernel_spmd(nc, in_maps, core_ids=list(range(NCORE)))
    out = np.empty((1, 8192, 2048), dtype=np.float32)
    for c in range(NCORE):
        out[0, c * 1024:(c + 1) * 1024, :] = np.asarray(res.results[c]["xout"]).T
    return out


# version 05
# speedup vs baseline: 1.0081x; 1.0081x over previous
import numpy as np
from contextlib import ExitStack

import concourse.bass as bass
import concourse.mybir as mybir
from concourse.bass_utils import run_bass_kernel_spmd

F32 = mybir.dt.float32
BF16 = mybir.dt.bfloat16
I8 = mybir.dt.int8
AF = mybir.ActivationFunctionType
ALU = mybir.AluOpType

NCORE = 8
H = 16
C = 2048
KC = 16
T = 1024
NI = 64
RND = 4
IPR = 16
NCH = 8
HHALF = 8


class Eng:

    def __init__(self, name):
        self.name = name
        self.sem = None
        self.n = 0
        self.q = []

    def wait(self, other, val):
        if val is None or val <= 0:
            return
        self.q.append(lambda E, s=other.sem, v=val: E.wait_ge(s, v))

    def do(self, fn, inc=False, incby=1):
        if inc:
            self.n += incby
            self.q.append(lambda E, fn=fn, ib=incby, s=self: fn(E).then_inc(s.sem, ib))
            return self.n
        self.q.append(lambda E, fn=fn: fn(E))
        return None


def build_program(upto="all"):
    nc = bass.Bass("TRN2", target_bir_lowering=False, detect_race_conditions=False)

    def din(name, shape, dty):
        return nc.dram_tensor(name, list(shape), dty, kind="ExternalInput").ap()

    xin = din("xin", [C, T], F32)
    wqkvt = din("wqkvt", [48, 128, 2048], F32)
    wgt = din("wgt", [NI, 128, 2048], I8)
    wut = din("wut", [NI, 128, 2048], I8)
    wdt = din("wdt", [RND, KC, 128, 2048], I8)
    woutt = din("woutt", [2, KC, 128, 1024], F32)
    bqkv = din("bqkv", [6144], F32)
    sgate = din("sgate", [8192], F32)
    sup = din("sup", [8192], F32)
    sdown = din("sdown", [2048], F32)
    boutv = din("boutv", [2048], F32)
    slopes = din("slopes", [16], F32)
    ident8 = din("ident8", [128, 128], I8)
    identf = din("identf", [128, 128], F32)
    d0c = din("d0c", [128, 128], F32)
    m01c = din("m01c", [128, 128], F32)
    dqc = din("dqc", [128, 128], F32)
    dk16c = din("dk16c", [128, 16], F32)
    dbd8c = din("dbd8c", [128, 128], F32)
    dcrwc = din("dcrwc", [128, 128], F32)
    zcc = din("zcc", [128, 128], F32)

    xout = nc.dram_tensor("xout", [C, T], F32, kind="ExternalOutput").ap()
    agin = [nc.dram_tensor(f"agin{i}", [HHALF * 128, 128], F32) for i in range(2)]
    agout = [nc.dram_tensor(f"agout{i}", [NCORE * HHALF * 128, 128], F32) for i in range(2)]

    TEN, VEC, SCA, GPS, SYN = (Eng(n) for n in ("tensor", "vector", "scalar", "gpsimd", "sync"))
    ENGS = dict(tensor=TEN, vector=VEC, scalar=SCA, gpsimd=GPS, sync=SYN)

    with ExitStack() as top:
        for e in ENGS.values():
            e.sem = top.enter_context(nc.semaphore(f"sem_{e.name}"))
        sem_c = [top.enter_context(nc.semaphore(f"sem_c{i}")) for i in range(2)]

        sb = lambda name, shape, dty: top.enter_context(nc.sbuf_tensor(name, list(shape), dty))

        xT = sb("xT", [128, KC * T], F32)
        xnT = sb("xnT", [128, KC * T], BF16)
        wA = [sb(f"wA{i}", [128, 2048], BF16) for i in range(2)]
        wB = [sb(f"wB{i}", [128, 2048], BF16) for i in range(2)]
        wC = [sb(f"wC{i}", [128, 2048], BF16) for i in range(2)]
        identb_s = sb("identb_s", [128, 128], BF16)
        masks = sb("masks", [128, H * 128], BF16)
        qdec = sb("qdec", [128, H * 128], BF16)
        kdecs = sb("kdecs", [128, 16], F32)
        bds = sb("bds", [128, 16], F32)
        bdpow = sb("bdpow", [128, 128], F32)
        rw = sb("rw", [128, 128], F32)
        biasA = sb("biasA", [128, 128], F32)
        biasB = sb("biasB", [128, 80], F32)
        sdh = sb("sdh", [128, 16], F32)
        slope_bc = sb("slope_bc", [128, 16], F32)
        ones_b = sb("ones_b", [128, 1], BF16)
        ones_f = sb("ones_f", [1, 128], F32)
        eps_t = sb("eps_t", [128, 1], F32)
        rowA = sb("rowA", [1, T], F32)
        rowB = sb("rowB", [1, T], F32)
        rowC = sb("rowC", [1, T], F32)
        row_m, row_mn, row_msq = rowA, rowB, rowC
        row_var, row_std, row_rstd = rowA, rowC, rowA

        rel = {}

        def acq(eng, slot):
            r = rel.get(slot)
            if r is not None:
                eng.wait(r[0], r[1])

        def release(slot, eng, val):
            rel[slot] = (eng, val)

        def flush():
            for name, e in ENGS.items():
                fns = e.q
                e.q = []
                if fns:
                    getattr(block, name)(lambda E, fns=fns: [f(E) for f in fns])

        def gdma(out, in_, slot=None):
            if slot is not None:
                acq(GPS, slot)
            return GPS.do(lambda E, o=out, i=in_: E.dma_start(out=o, in_=i), inc=True, incby=16)

        def ydma(out, in_):
            return SYN.do(lambda E, o=out, i=in_: E.dma_start(out=o, in_=i), inc=True, incby=16)

        block = top.enter_context(nc.Block())

        with ExitStack() as sctx:
            ssb = lambda name, shape, dty: sctx.enter_context(nc.sbuf_tensor(name, list(shape), dty))
            identf_s = ssb("identf_s", [128, 128], F32)
            d0_s = ssb("d0_s", [128, 128], F32)
            m01_s = ssb("m01_s", [128, 128], F32)
            dq_s = ssb("dq_s", [128, 128], F32)
            dk16_s = ssb("dk16_s", [128, 16], F32)
            dbd8_s = ssb("dbd8_s", [128, 128], F32)
            dcrw_s = ssb("dcrw_s", [128, 128], F32)
            zc_s = ssb("zc_s", [128, 128], F32)
            slope_rh = ssb("slope_rh", [128, 128], F32)
            slope_h8 = ssb("slope_h8", [128, 128], F32)
            stage1 = ssb("stage1", [128, 128], F32)
            stage2 = ssb("stage2", [128, 128], F32)
            mexp = [ssb(f"mexp{i}", [128, 128], F32) for i in range(2)]
            pst1 = sctx.enter_context(nc.psum_tensor("pst1", [128, 512], F32))
            pst2 = sctx.enter_context(nc.psum_tensor("pst2", [128, 512], F32))

            v_xload_k = []
            for k in range(KC):
                v_xload_k.append(ydma(xT[:, k * T:(k + 1) * T], xin[k * 128:(k + 1) * 128, :]))
            v_xload = v_xload_k[-1]
            for dst, src in ((identf_s, identf), (d0_s, d0c), (m01_s, m01c), (dq_s, dqc),
                             (dk16_s, dk16c), (dbd8_s, dbd8c), (dcrw_s, dcrwc), (zc_s, zcc)):
                ydma(dst[:, :], src)
            ydma(slope_bc[:, :], slopes[None, :].broadcast_to([128, 16]))
            ydma(slope_rh[:, :].rearrange("p (r h) -> p r h", r=8),
                 slopes[None, None, :].broadcast_to([128, 8, 16]))
            ydma(slope_h8[:, :].rearrange("p (t h) -> p t h", t=8),
                 slopes[None, None, :].broadcast_to([128, 8, 16]))
            ydma(stage1[0:48, :], bqkv.rearrange("(a b) -> a b", b=128))
            ydma(stage1[48:112, :], sgate.rearrange("(a b) -> a b", b=128))
            ydma(stage1[112:128, :], sdown.rearrange("(a b) -> a b", b=128))
            ydma(stage2[0:64, :], sup.rearrange("(a b) -> a b", b=128))
            v_sload = ydma(stage2[64:80, :], boutv.rearrange("(a b) -> a b", b=128))
            gdma(identb_s[:, :], ident8)

            TEN.wait(SYN, v_sload)
            t_tr1 = TEN.do(lambda E: E.transpose(pst1[:, 0:128], stage1[:, :], identf_s[:, :]), inc=True)
            t_tr2 = TEN.do(lambda E: E.transpose(pst2[:, 0:128], stage2[:, :], identf_s[:, :]), inc=True)
            VEC.wait(TEN, t_tr1)
            VEC.do(lambda E: E.tensor_copy(biasA[:, :], pst1[:, 0:128]))
            VEC.wait(TEN, t_tr2)
            v_bB = VEC.do(lambda E: E.tensor_copy(biasB[:, :], pst2[:, 0:80]), inc=True)
            VEC.wait(VEC, v_bB)
            VEC.do(lambda E: E.tensor_scalar_mul(sdh[:, :], biasA[:, 112:128], 0.5))
            VEC.do(lambda E: E.memset(ones_b[:, :], 1.0))
            VEC.do(lambda E: E.memset(ones_f[:, :], 1.0))
            VEC.do(lambda E: E.memset(eps_t[:, :], 1e-5))

            SCA.wait(SYN, v_sload)
            s_bds = SCA.do(lambda E: E.activation(bds[:, :], slope_bc[:, :], AF.Exp, scale=-128.0), inc=True)
            VEC.do(lambda E: E.tensor_mul(mexp[0][:, 0:16], dk16_s[:, :], slope_bc[:, :]))
            v_kd = VEC.do(lambda E: E.tensor_mul(mexp[1][:, :], dbd8_s[:, :], slope_h8[:, :]), inc=True)
            SCA.wait(VEC, v_kd)
            SCA.do(lambda E: E.activation(kdecs[:, :], mexp[0][:, 0:16], AF.Exp))
            s_bdp = SCA.do(lambda E: E.activation(bdpow[:, :], mexp[1][:, :], AF.Exp), inc=True)
            VEC.wait(SCA, s_bdp)
            v_rw1 = VEC.do(lambda E: E.tensor_mul(mexp[0][:, :], dcrw_s[:, :], slope_rh[:, :]), inc=True)
            SCA.wait(VEC, v_rw1)
            SCA.wait(SCA, s_bdp)
            s_rw = SCA.do(lambda E: E.activation(mexp[1][:, :], mexp[0][:, :], AF.Exp), inc=True)
            VEC.wait(SCA, s_rw)
            VEC.do(lambda E: E.tensor_mul(rw[:, :], mexp[1][:, :], zc_s[:, :]))
            for h in range(H):
                sc = slope_bc[:, h:h + 1]
                s_m = SCA.do(lambda E, sc=sc, h=h: E.activation(
                    mexp[h % 2][:, :], d0_s[:, :], AF.Exp, scale=sc), inc=True)
                VEC.wait(SCA, s_m)
                v_m = VEC.do(lambda E, h=h: E.tensor_mul(
                    masks[:, h * 128:(h + 1) * 128], mexp[h % 2][:, :], m01_s[:, :]), inc=True)
                SCA.wait(VEC, v_m)
                SCA.do(lambda E, sc=sc, h=h: E.activation(
                    qdec[:, h * 128:(h + 1) * 128], dq_s[:, :], AF.Exp, scale=sc))
            flush()

        def layer_norm(tag, xready_eng, xready_val, sub="all", xready_k=None):
            with ExitStack() as ctx:
                ps1 = ctx.enter_context(nc.psum_tensor(f"ps1{tag}", [128, T], F32))
                ps2 = ctx.enter_context(nc.psum_tensor(f"ps2{tag}", [128, T], F32))
                psa = ctx.enter_context(nc.psum_tensor(f"psa{tag}", [128, T], F32))
                psb = ctx.enter_context(nc.psum_tensor(f"psb{tag}", [128, T], F32))
                xb2 = [ctx.enter_context(nc.sbuf_tensor(f"xb{tag}{i}", [128, T], BF16)) for i in range(2)]
                sq2 = [ctx.enter_context(nc.sbuf_tensor(f"sq{tag}{i}", [128, T], BF16)) for i in range(2)]
                mm_done = [0] * KC
                for k in range(KC):
                    SCA.wait(xready_eng, xready_k[k] if xready_k else (xready_val if k == 0 else 0))
                    acq(SCA, f"xb{k % 2}")
                    xb_done = SCA.do(lambda E, k=k: E.activation(
                        xb2[k % 2][:, :], xT[:, k * T:(k + 1) * T], AF.Copy), inc=True)
                    VEC.wait(SCA, xb_done)
                    acq(VEC, f"sq{k % 2}")
                    sq_done = VEC.do(lambda E, k=k: E.tensor_mul(
                        sq2[k % 2][:, :], xb2[k % 2][:, :], xb2[k % 2][:, :]), inc=True)
                    TEN.wait(SCA, xb_done)
                    TEN.wait(VEC, sq_done)
                    if k == 0:
                        acq(TEN, "ln_psum")
                    st, sp = (k == 0), (k == KC - 1)
                    for half in range(2):
                        TEN.do(lambda E, k=k, half=half, st=st, sp=sp: E.matmul(
                            ps1[0:1, half * 512:(half + 1) * 512], ones_b[:, :],
                            xb2[k % 2][:, half * 512:(half + 1) * 512], start=st, stop=sp))
                    TEN.do(lambda E, k=k, st=st, sp=sp: E.matmul(
                        ps2[0:1, 0:512], ones_b[:, :], sq2[k % 2][:, 0:512], start=st, stop=sp))
                    mm_done[k] = TEN.do(lambda E, k=k, st=st, sp=sp: E.matmul(
                        ps2[0:1, 512:1024], ones_b[:, :], sq2[k % 2][:, 512:1024],
                        start=st, stop=sp), inc=True)
                    release(f"xb{k % 2}", TEN, mm_done[k])
                    release(f"sq{k % 2}", TEN, mm_done[k])
                VEC.wait(TEN, mm_done[KC - 1])
                v_r1 = VEC.do(lambda E: E.tensor_scalar_mul(row_m[:, :], ps1[0:1, :], 1.0 / C), inc=True)
                VEC.wait(VEC, v_r1)
                VEC.do(lambda E: E.tensor_scalar_mul(row_mn[:, :], row_m[:, :], -1.0))
                v_r2 = VEC.do(lambda E: E.tensor_mul(row_msq[:, :], row_m[:, :], row_m[:, :]), inc=True)
                VEC.wait(VEC, v_r2)
                v_r3 = VEC.do(lambda E: E.tensor_scalar_mul(row_msq[:, :], row_msq[:, :], -1.0), inc=True)
                VEC.wait(VEC, v_r3)
                v_var = VEC.do(lambda E: E.scalar_tensor_tensor(
                    out=row_var[:, :], in0=ps2[0:1, :], scalar=1.0 / C, in1=row_msq[:, :],
                    op0=ALU.mult, op1=ALU.add), inc=True)
                SCA.wait(VEC, v_var)
                s_std = SCA.do(lambda E: E.activation(row_std[:, :], row_var[:, :], AF.Sqrt, bias=eps_t[0:1, 0:1]), inc=True)
                VEC.wait(SCA, s_std)
                v_rstd = VEC.do(lambda E: E.reciprocal(row_rstd[:, :], row_std[:, :]), inc=True)
                TEN.wait(VEC, v_rstd)
                for half in range(2):
                    TEN.do(lambda E, half=half: E.matmul(
                        psa[:, half * 512:(half + 1) * 512], ones_f[:, :],
                        row_rstd[0:1, half * 512:(half + 1) * 512], start=True, stop=True))
                TEN.do(lambda E: E.matmul(
                    psb[:, 0:512], ones_f[:, :], row_mn[0:1, 0:512], start=True, stop=True))
                t_bc = TEN.do(lambda E: E.matmul(
                    psb[:, 512:1024], ones_f[:, :], row_mn[0:1, 512:1024],
                    start=True, stop=True), inc=True)
                VEC.wait(TEN, t_bc)
                v_xn = 0
                v_xnk = []
                for k in range(KC):
                    v_n1 = VEC.do(lambda E, k=k: E.tensor_tensor(
                        xnT[:, k * T:(k + 1) * T], xT[:, k * T:(k + 1) * T], psb[:, :], ALU.add),
                        inc=True)
                    VEC.wait(VEC, v_n1)
                    v_xn = VEC.do(lambda E, k=k: E.tensor_tensor(
                        xnT[:, k * T:(k + 1) * T], xnT[:, k * T:(k + 1) * T], psa[:, :], ALU.mult),
                        inc=True)
                    v_xnk.append(v_xn)
                release("ln_psum", VEC, v_xn)
                flush()
                return v_xn, v_xnk

        def ffn_pass(pass_idx, v_xnk):
            with ExitStack() as ctx:
                fuse = ctx.enter_context(nc.sbuf_tensor(f"fuse{pass_idx}", [128, IPR * T], BF16))
                gtmp = [ctx.enter_context(nc.sbuf_tensor(f"gt{pass_idx}_{i}", [128, T], BF16)) for i in range(2)]
                utmp = [ctx.enter_context(nc.sbuf_tensor(f"ut{pass_idx}_{i}", [128, T], BF16)) for i in range(2)]
                pg = [ctx.enter_context(nc.psum_tensor(f"pg{pass_idx}_{i}", [128, T], F32)) for i in range(2)]
                pu = [ctx.enter_context(nc.psum_tensor(f"pu{pass_idx}_{i}", [128, T], F32)) for i in range(2)]
                psy = pg + pu
                pskey = ["pg0", "pg1", "pu0", "pu1"]
                for r in range(RND):
                    v_fuse_round = 0
                    for ii in range(IPR):
                        gi = r * IPR + ii
                        nwg = gdma(wA[gi % 2][:, :], wgt[gi], slot=f"wA{gi % 2}")
                        nwu = gdma(wB[gi % 2][:, :], wut[gi], slot=f"wB{gi % 2}")
                        TEN.wait(GPS, nwg)
                        acq(TEN, f"pg{gi % 2}")
                        t_g = 0
                        for k in range(KC):
                            if gi == 0:
                                TEN.wait(VEC, v_xnk[k])
                            for half in range(2):
                                last = (k == KC - 1 and half == 1)
                                t_g = TEN.do(lambda E, gi=gi, half=half, k=k: E.matmul(
                                    pg[gi % 2][:, half * 512:(half + 1) * 512],
                                    wA[gi % 2][:, k * 128:(k + 1) * 128],
                                    xnT[:, k * T + half * 512: k * T + (half + 1) * 512],
                                    start=(k == 0), stop=(k == KC - 1)), inc=last)
                        release(f"wA{gi % 2}", TEN, t_g)
                        TEN.wait(GPS, nwu)
                        acq(TEN, f"pu{gi % 2}")
                        t_u = 0
                        for k in range(KC):
                            for half in range(2):
                                last = (k == KC - 1 and half == 1)
                                t_u = TEN.do(lambda E, gi=gi, half=half, k=k: E.matmul(
                                    pu[gi % 2][:, half * 512:(half + 1) * 512],
                                    wB[gi % 2][:, k * 128:(k + 1) * 128],
                                    xnT[:, k * T + half * 512: k * T + (half + 1) * 512],
                                    start=(k == 0), stop=(k == KC - 1)), inc=last)
                        release(f"wB{gi % 2}", TEN, t_u)
                        SCA.wait(TEN, t_g)
                        acq(SCA, f"gtmp{gi % 2}")
                        SCA.do(lambda E, gi=gi: E.activation(
                            gtmp[gi % 2][:, :], pg[gi % 2][:, :], AF.Silu,
                            scale=biasA[:, 48 + gi % 64:49 + gi % 64]))
                        SCA.wait(TEN, t_u)
                        acq(SCA, f"utmp{gi % 2}")
                        s_e = SCA.do(lambda E, gi=gi: E.activation(
                            utmp[gi % 2][:, :], pu[gi % 2][:, :], AF.Copy,
                            scale=biasB[:, gi % 64:gi % 64 + 1]), inc=True)
                        release(f"pg{gi % 2}", SCA, s_e)
                        release(f"pu{gi % 2}", SCA, s_e)
                        VEC.wait(SCA, s_e)
                        acq(VEC, f"fuse{ii}")
                        v_f = VEC.do(lambda E, gi=gi, ii=ii: E.tensor_mul(
                            fuse[:, ii * T:(ii + 1) * T], gtmp[gi % 2][:, :], utmp[gi % 2][:, :]), inc=True)
                        release(f"gtmp{gi % 2}", VEC, v_f)
                        release(f"utmp{gi % 2}", VEC, v_f)
                        if ii == IPR - 1:
                            v_fuse_round = v_f
                    t_y_last = 0
                    for m in range(KC):
                        nwd = gdma(wC[m % 2][:, :], wdt[r, m], slot=f"wC{m % 2}")
                        TEN.wait(VEC, v_fuse_round)
                        TEN.wait(GPS, nwd)
                        acq(TEN, pskey[m % 4])
                        t_y = 0
                        for ii in range(IPR):
                            for half in range(2):
                                last = (ii == IPR - 1 and half == 1)
                                t_y = TEN.do(lambda E, m=m, half=half, ii=ii: E.matmul(
                                    psy[m % 4][:, half * 512:(half + 1) * 512],
                                    wC[m % 2][:, ii * 128:(ii + 1) * 128],
                                    fuse[:, ii * T + half * 512: ii * T + (half + 1) * 512],
                                    start=(ii == 0), stop=(ii == IPR - 1)), inc=last)
                        release(f"wC{m % 2}", TEN, t_y)
                        t_y_last = t_y
                        VEC.wait(TEN, t_y)
                        v_mg = VEC.do(lambda E, m=m: E.scalar_tensor_tensor(
                            out=xT[:, m * T:(m + 1) * T], in0=psy[m % 4][:, :],
                            scalar=sdh[:, m:m + 1], in1=xT[:, m * T:(m + 1) * T],
                            op0=ALU.mult, op1=ALU.add), inc=True)
                        release(pskey[m % 4], VEC, v_mg)
                        if pass_idx == 1 and r == RND - 1:
                            SYN.wait(VEC, v_mg)
                            ydma(xout[m * 128:(m + 1) * 128, :], xT[:, m * T:(m + 1) * T])
                    for ii in range(IPR):
                        release(f"fuse{ii}", TEN, t_y_last)
                v_last_merge = VEC.n
                flush()
                return v_last_merge

        def attn_half(half_idx, v_xn, v_xnk, o_sb, qhat, hb, sbufs, prev_done):
            qTh, kTh, vT_scr, vtok = hb
            scb, qt_scr, kt_scr, Sb, Sf = sbufs
            h0 = half_idx * HHALF
            with ExitStack() as ctx:
                pJ0 = ctx.enter_context(nc.psum_tensor(f"pJ0{half_idx}", [128, T], F32))
                pJ1 = ctx.enter_context(nc.psum_tensor(f"pJ1{half_idx}", [128, T], F32))
                pQ, pK, pV = pJ0, pJ1, pJ0
                pscS_t = ctx.enter_context(nc.psum_tensor(f"pscS{half_idx}", [128, 512], F32))
                po_t = ctx.enter_context(nc.psum_tensor(f"po{half_idx}", [128, 512], F32))
                psc, pS, po = pscS_t[:, 0:128], pscS_t[:, 256:384], po_t[:, 0:128]
                pbkA = ctx.enter_context(nc.psum_tensor(f"pbkA{half_idx}", [128, 1024], BF16))
                pbkB = ctx.enter_context(nc.psum_tensor(f"pbkB{half_idx}", [128, 1024], BF16))
                pktr = pbkA[:, 0:128]
                ptrv = [pbkA[:, 128:256], pbkB[:, 0:128]]

                n_ast = 0
                for hh in range(HHALF):
                    h = h0 + hh
                    nq = gdma(wB[hh % 2][:, :], wqkvt[h], slot=f"wB{hh % 2}")
                    nk = gdma(wC[hh % 2][:, :], wqkvt[16 + h], slot=f"wC{hh % 2}")
                    nv = gdma(wA[hh % 2][:, :], wqkvt[32 + h], slot=f"wA{hh % 2}")
                    if hh == 0:
                        TEN.wait(VEC, v_xn)
                        if prev_done:
                            TEN.wait(VEC, prev_done)
                    TEN.wait(GPS, nq)
                    acq(TEN, "pQ")
                    t_q = 0
                    for k in range(KC):
                        if hh == 0:
                            TEN.wait(VEC, v_xnk[k])
                        for half in range(2):
                            last = (k == KC - 1 and half == 1)
                            t_q = TEN.do(lambda E, hh=hh, half=half, k=k: E.matmul(
                                pQ[:, half * 512:(half + 1) * 512],
                                wB[hh % 2][:, k * 128:(k + 1) * 128],
                                xnT[:, k * T + half * 512: k * T + (half + 1) * 512],
                                start=(k == 0), stop=(k == KC - 1)), inc=last)
                    release(f"wB{hh % 2}", TEN, t_q)
                    TEN.wait(GPS, nk)
                    acq(TEN, "pK")
                    t_k = 0
                    for k in range(KC):
                        for half in range(2):
                            last = (k == KC - 1 and half == 1)
                            t_k = TEN.do(lambda E, hh=hh, half=half, k=k: E.matmul(
                                pK[:, half * 512:(half + 1) * 512],
                                wC[hh % 2][:, k * 128:(k + 1) * 128],
                                xnT[:, k * T + half * 512: k * T + (half + 1) * 512],
                                start=(k == 0), stop=(k == KC - 1)), inc=last)
                    release(f"wC{hh % 2}", TEN, t_k)
                    SCA.wait(TEN, t_q)
                    acq(SCA, "qTh")
                    s_evq = SCA.do(lambda E, h=h: E.activation(
                        qTh[:, :], pQ[:, :], AF.Identity, bias=biasA[:, h:h + 1]), inc=True)
                    SCA.wait(TEN, t_k)
                    acq(SCA, "kTh")
                    s_evk = SCA.do(lambda E, h=h: E.activation(
                        kTh[:, :], pK[:, :], AF.Identity, bias=biasA[:, 16 + h:17 + h]), inc=True)
                    release("pK", SCA, s_evk)
                    TEN.wait(GPS, nv)
                    TEN.wait(SCA, s_evq)
                    t_v = 0
                    for k in range(KC):
                        for half in range(2):
                            last = (k == KC - 1 and half == 1)
                            t_v = TEN.do(lambda E, hh=hh, half=half, k=k: E.matmul(
                                pV[:, half * 512:(half + 1) * 512],
                                wA[hh % 2][:, k * 128:(k + 1) * 128],
                                xnT[:, k * T + half * 512: k * T + (half + 1) * 512],
                                start=(k == 0), stop=(k == KC - 1)), inc=last)
                    release(f"wA{hh % 2}", TEN, t_v)
                    SCA.wait(TEN, t_v)
                    acq(SCA, "vT_scr")
                    s_qkv = SCA.do(lambda E, h=h: E.activation(
                        vT_scr[:, :], pV[:, :], AF.Identity, bias=biasA[:, 32 + h:33 + h]), inc=True)
                    release("pQ", SCA, s_qkv)
                    TEN.wait(SCA, s_qkv)
                    v_vtok = 0
                    t_tr = 0
                    for t in range(NCH):
                        acq(TEN, f"ptrv{t % 2}")
                        t_tr = TEN.do(lambda E, t=t: E.transpose(
                            ptrv[t % 2], vT_scr[:, t * 128:(t + 1) * 128], identb_s[:, :]), inc=True)
                        VEC.wait(TEN, t_tr)
                        acq(VEC, f"vtok{t}")
                        v_vtok = VEC.do(lambda E, t=t: E.tensor_copy(
                            vtok[:, t * 128:(t + 1) * 128], ptrv[t % 2]), inc=True)
                        release(f"ptrv{t % 2}", VEC, v_vtok)
                    release("vT_scr", TEN, t_tr)
                    acq(VEC, f"Sf{hh % 2}")
                    acq(VEC, f"Sfv{hh % 2}")
                    VEC.do(lambda E, hh=hh: E.memset(Sf[hh % 2][:, :], 0.0))
                    v_su = 0
                    v_qt = 0
                    t_kt = 0
                    for t in range(NCH):
                        if t == 0:
                            TEN.wait(VEC, v_vtok)
                        acq(TEN, "psc")
                        t_sc = TEN.do(lambda E, t=t: E.matmul(
                            psc, kTh[:, t * 128:(t + 1) * 128], qTh[:, t * 128:(t + 1) * 128],
                            start=True, stop=True), inc=True)
                        VEC.wait(TEN, t_sc)
                        acq(VEC, "scb")
                        v_sc = VEC.do(lambda E, h=h: E.tensor_mul(
                            scb[:, :], psc, masks[:, h * 128:(h + 1) * 128]), inc=True)
                        acq(VEC, "qt_scr")
                        v_q0 = VEC.do(lambda E, h=h, t=t: E.tensor_mul(
                            qt_scr[:, :], qTh[:, t * 128:(t + 1) * 128], qdec[:, h * 128:(h + 1) * 128]),
                            inc=True)
                        VEC.wait(VEC, v_q0)
                        v_qt = VEC.do(lambda E, h=h, hh=hh, t=t: E.tensor_scalar_mul(
                            qhat[:, hh * T + t * 128: hh * T + (t + 1) * 128], qt_scr[:, :],
                            bdpow[:, t * 16 + h: t * 16 + h + 1]), inc=True)
                        TEN.wait(VEC, v_qt)
                        acq(TEN, "po")
                        if t == 0:
                            t_o = TEN.do(lambda E, t=t: E.matmul(
                                po, vtok[:, t * 128:(t + 1) * 128], scb[:, :],
                                start=True, stop=True), inc=True)
                        else:
                            TEN.do(lambda E, t=t: E.matmul(
                                po, vtok[:, t * 128:(t + 1) * 128], scb[:, :],
                                start=True, stop=False))
                            t_o = TEN.do(lambda E: E.matmul(
                                po, Sb[:, :], qt_scr[:, :], start=False, stop=True), inc=True)
                        release("scb", TEN, t_o)
                        release("qt_scr", TEN, t_o)
                        acq(TEN, "pktr")
                        t_kt = TEN.do(lambda E, t=t: E.transpose(
                            pktr, kTh[:, t * 128:(t + 1) * 128], identb_s[:, :]), inc=True)
                        SCA.wait(TEN, t_kt)
                        acq(SCA, "kt_scr")
                        s_kt = SCA.do(lambda E, h=h: E.activation(
                            kt_scr[:, :], pktr, AF.Copy, scale=kdecs[:, h:h + 1]), inc=True)
                        release("pktr", SCA, s_kt)
                        TEN.wait(SCA, s_kt)
                        acq(TEN, "pS")
                        t_S = TEN.do(lambda E, t=t: E.matmul(
                            pS, kt_scr[:, :], vtok[:, t * 128:(t + 1) * 128],
                            start=True, stop=True), inc=True)
                        release("kt_scr", TEN, t_S)
                        VEC.wait(TEN, t_S)
                        v_su = VEC.do(lambda E, h=h, hh=hh: E.scalar_tensor_tensor(
                            out=Sf[hh % 2][:, :], in0=Sf[hh % 2][:, :], scalar=bds[:, h:h + 1],
                            in1=pS, op0=ALU.mult, op1=ALU.add), inc=True)
                        release("pS", VEC, v_su)
                        release("psc", VEC, v_su)
                        if t < NCH - 1:
                            VEC.wait(TEN, t_o)
                            VEC.wait(VEC, v_su)
                            VEC.do(lambda E, hh=hh: E.tensor_copy(Sb[:, :], Sf[hh % 2][:, :]))
                        SCA.wait(TEN, t_o)
                        s_oev = SCA.do(lambda E, hh=hh, t=t: E.activation(
                            o_sb[:, hh * T + t * 128: hh * T + (t + 1) * 128], po, AF.Copy), inc=True)
                        release("po", SCA, s_oev)
                    SYN.wait(VEC, v_su)
                    n_ast = ydma(agin[half_idx].ap()[hh * 128:(hh + 1) * 128, :], Sf[hh % 2][:, :])
                    release(f"Sf{hh % 2}", SYN, n_ast)
                    release(f"Sfv{hh % 2}", VEC, v_su)
                    release("qTh", VEC, v_qt)
                    release("kTh", TEN, t_kt)
                    for t in range(NCH):
                        release(f"vtok{t}", TEN, t_S)
                GPS.wait(SYN, n_ast)
                GPS.do(lambda E, hi=half_idx, cs=sem_c[half_idx]: E.collective_compute(
                    "AllGather", ALU.bypass, replica_groups=[list(range(NCORE))],
                    ins=[agin[hi].ap().opt()], outs=[agout[hi].ap().opt()]).then_inc(cs, 1))
                scan_end = dict(t=TEN.n, v=VEC.n, s=SCA.n)
                flush()
                return scan_end

        def attn_global(half_idx, o_sb, qhat, scan_end, agl2, S_in, S_in_b):
            h0 = half_idx * HHALF
            with ExitStack() as ctx:
                pog = [ctx.enter_context(nc.psum_tensor(f"pog{half_idx}_{i}", [128, T], F32)) for i in range(2)]
                pgo = [ctx.enter_context(nc.psum_tensor(f"pgo{half_idx}_{i}", [128, T], F32)) for i in range(2)]
                SYN.q.append(lambda E, cs=sem_c[half_idx]: E.wait_ge(cs, 1))
                SYN.wait(TEN, scan_end["t"])
                SYN.wait(VEC, scan_end["v"])
                SYN.wait(SCA, scan_end["s"])
                v_cmb = 0
                for r in range(NCORE):
                    acq(SYN, f"agl{r % 2}")
                    n_l = ydma(agl2[r % 2][:, :].rearrange("d (h e) -> d h e", h=HHALF),
                               agout[half_idx].ap()[r * HHALF * 128:(r + 1) * HHALF * 128, :]
                               .rearrange("(h d) e -> d h e", h=HHALF))
                    VEC.wait(SYN, n_l)
                    for hh in range(HHALF):
                        h = h0 + hh
                        if r == 0:
                            if hh == 0:
                                acq(VEC, "S_in_cast")
                            v_cmb = VEC.do(lambda E, r=r, hh=hh, h=h: E.tensor_scalar_mul(
                                S_in[:, hh * 128:(hh + 1) * 128],
                                agl2[r % 2][:, hh * 128:(hh + 1) * 128],
                                rw[:, r * 16 + h: r * 16 + h + 1]), inc=(hh == HHALF - 1))
                        else:
                            v_cmb = VEC.do(lambda E, r=r, hh=hh, h=h: E.scalar_tensor_tensor(
                                out=S_in[:, hh * 128:(hh + 1) * 128],
                                in0=agl2[r % 2][:, hh * 128:(hh + 1) * 128],
                                scalar=rw[:, r * 16 + h: r * 16 + h + 1],
                                in1=S_in[:, hh * 128:(hh + 1) * 128],
                                op0=ALU.mult, op1=ALU.add), inc=(hh == HHALF - 1))
                    release(f"agl{r % 2}", VEC, v_cmb)
                SCA.wait(VEC, v_cmb)
                acq(SCA, "S_in_b")
                s_sib = SCA.do(lambda E: E.activation(S_in_b[:, :], S_in[:, :], AF.Copy), inc=True)
                release("S_in_cast", SCA, s_sib)
                t_og = 0
                v_oadd = 0
                for hh in range(HHALF):
                    if hh == 0:
                        TEN.wait(SCA, s_sib)
                        VEC.wait(SCA, scan_end["s"])
                    acq(TEN, f"pog{hh % 2}")
                    for half in range(2):
                        t_og = TEN.do(lambda E, hh=hh, half=half: E.matmul(
                            pog[hh % 2][:, half * 512:(half + 1) * 512],
                            S_in_b[:, hh * 128:(hh + 1) * 128],
                            qhat[:, hh * T + half * 512: hh * T + (half + 1) * 512],
                            start=True, stop=True), inc=(half == 1))
                    VEC.wait(TEN, t_og)
                    v_oadd = VEC.do(lambda E, hh=hh: E.tensor_tensor(
                        o_sb[:, hh * T:(hh + 1) * T], o_sb[:, hh * T:(hh + 1) * T],
                        pog[hh % 2][:, :], ALU.add), inc=True)
                    release(f"pog{hh % 2}", VEC, v_oadd)
                release("S_in_b", TEN, t_og)
                v_m2 = 0
                for m in range(KC):
                    nw = gdma(wA[m % 2][:, 0:1024], woutt[half_idx, m], slot=f"wA{m % 2}")
                    if m == 0:
                        TEN.wait(VEC, v_oadd)
                    TEN.wait(GPS, nw)
                    acq(TEN, f"pgo{m % 2}")
                    t_po = 0
                    for hh in range(HHALF):
                        for half in range(2):
                            last = (hh == HHALF - 1 and half == 1)
                            t_po = TEN.do(lambda E, m=m, half=half, hh=hh: E.matmul(
                                pgo[m % 2][:, half * 512:(half + 1) * 512],
                                wA[m % 2][:, hh * 128:(hh + 1) * 128],
                                o_sb[:, hh * T + half * 512: hh * T + (half + 1) * 512],
                                start=(hh == 0), stop=(hh == HHALF - 1)), inc=last)
                    release(f"wA{m % 2}", TEN, t_po)
                    VEC.wait(TEN, t_po)
                    if half_idx == 0:
                        v_m2 = VEC.do(lambda E, m=m: E.scalar_tensor_tensor(
                            out=xT[:, m * T:(m + 1) * T], in0=pgo[m % 2][:, :],
                            scalar=biasB[:, 64 + m:65 + m], in1=xT[:, m * T:(m + 1) * T],
                            op0=ALU.add, op1=ALU.add), inc=True)
                    else:
                        v_m2 = VEC.do(lambda E, m=m: E.tensor_tensor(
                            xT[:, m * T:(m + 1) * T], xT[:, m * T:(m + 1) * T],
                            pgo[m % 2][:, :], ALU.add), inc=True)
                    release(f"pgo{m % 2}", VEC, v_m2)
                flush()
                return v_m2

        def _bail(ready_eng, ready_val):
            SYN.wait(ready_eng, ready_val)
            for m in range(KC):
                ydma(xout[m * 128:(m + 1) * 128, :], xT[:, m * T:(m + 1) * T])
            SYN.q.append(lambda E, s=SYN, v=SYN.n: E.wait_ge(s.sem, v))
            flush()

        if upto == "setup":
            _bail(SYN, v_xload)
            return nc
        v_xn1, v_xnk1 = layer_norm("a", SYN, v_xload, xready_k=v_xload_k)
        if upto == "ln1":
            _bail(VEC, v_xn1)
            return nc
        v_ffn1 = ffn_pass(0, v_xnk1)
        if upto == "ffn1":
            _bail(VEC, v_ffn1)
            return nc
        v_xn2, v_xnk2 = layer_norm("b", VEC, v_ffn1)
        if upto == "ln2":
            _bail(VEC, v_xn2)
            return nc

        with ExitStack() as actx:
            asb = lambda name, shape, dty: actx.enter_context(nc.sbuf_tensor(name, list(shape), dty))
            qTh = asb("qTh", [128, T], BF16)
            kTh = asb("kTh", [128, T], BF16)
            vT_scr = asb("vT_scr", [128, T], BF16)
            vtok = asb("vtok", [128, T], BF16)
            scb = asb("scb", [128, 128], BF16)
            qt_scr = asb("qt_scr", [128, 128], BF16)
            kt_scr = asb("kt_scr", [128, 128], BF16)
            Sb = asb("Sb", [128, 128], BF16)
            Sf = [asb(f"Sf{i}", [128, 128], F32) for i in range(2)]
            agl2 = [asb(f"agl{i}", [128, HHALF * 128], F32) for i in range(2)]
            S_in = asb("S_in", [128, HHALF * 128], F32)
            S_in_b = asb("S_in_b", [128, HHALF * 128], BF16)
            hb = (qTh, kTh, vT_scr, vtok)
            sbufs = (scb, qt_scr, kt_scr, Sb, Sf)

            with ExitStack() as h1ctx:
                o1 = h1ctx.enter_context(nc.sbuf_tensor("o1", [128, HHALF * T], BF16))
                qh1 = h1ctx.enter_context(nc.sbuf_tensor("qh1", [128, HHALF * T], BF16))
                se1 = attn_half(0, v_xn2, v_xnk2, o1, qh1, hb, sbufs, 0)
                if upto == "attn1":
                    _bail(SCA, se1["s"])
                    return nc
                v_g1 = attn_global(0, o1, qh1, se1, agl2, S_in, S_in_b)
                if upto == "attng1":
                    _bail(VEC, v_g1)
                    return nc
            with ExitStack() as h2ctx:
                o2 = h2ctx.enter_context(nc.sbuf_tensor("o2", [128, HHALF * T], BF16))
                qh2 = h2ctx.enter_context(nc.sbuf_tensor("qh2", [128, HHALF * T], BF16))
                se2 = attn_half(1, v_xn2, v_xnk2, o2, qh2, hb, sbufs, v_g1)
                if upto == "attn2":
                    _bail(SCA, se2["s"])
                    return nc
                v_g2 = attn_global(1, o2, qh2, se2, agl2, S_in, S_in_b)
                if upto == "attng2":
                    _bail(VEC, v_g2)
                    return nc

        v_xn3, v_xnk3 = layer_norm("c", VEC, v_g2)
        if upto == "ln3":
            _bail(VEC, v_xn3)
            return nc
        ffn_pass(1, v_xnk3)

        SYN.q.append(lambda E, s=SYN, v=SYN.n: E.wait_ge(s.sem, v))
        flush()

    return nc


_CACHE = {}


def _host_prep(L0_attn_x, L0_attn_Wqkv, L0_attn_bqkv, L0_attn_Wout, L0_attn_bout,
               L0_attn_slopes, L0_ffn_w_gate, L0_ffn_w_up, L0_ffn_w_down,
               L0_ffn_s_gate, L0_ffn_s_up, L0_ffn_s_down):
    f32 = np.float32
    x = np.asarray(L0_attn_x, f32)
    Wqkv = np.asarray(L0_attn_Wqkv, f32)
    Wout = np.asarray(L0_attn_Wout, f32)
    wg = np.asarray(L0_ffn_w_gate, np.int8)
    wu = np.asarray(L0_ffn_w_up, np.int8)
    wd = np.asarray(L0_ffn_w_down, np.int8)

    wqkvt = np.ascontiguousarray(
        Wqkv.reshape(16, 128, 48, 128).transpose(2, 1, 0, 3).reshape(48, 128, 2048))
    wgt = np.ascontiguousarray(
        wg.reshape(16, 128, 64, 128).transpose(2, 1, 0, 3).reshape(64, 128, 2048))
    wut = np.ascontiguousarray(
        wu.reshape(16, 128, 64, 128).transpose(2, 1, 0, 3).reshape(64, 128, 2048))
    wdt = np.ascontiguousarray(
        wd.reshape(4, 16, 128, 16, 128).transpose(0, 3, 2, 1, 4).reshape(4, 16, 128, 2048))
    woutt = np.ascontiguousarray(
        Wout.reshape(2, 8, 128, 16, 128).transpose(0, 3, 2, 1, 4).reshape(2, 16, 128, 1024))

    jj, ii = np.mgrid[0:128, 0:128]
    d0c = (-np.maximum(ii - jj, 0)).astype(f32)
    m01c = (ii >= jj).astype(f32)
    dqc = np.broadcast_to(-np.arange(128, dtype=f32)[None, :], (128, 128)).copy()
    dk16c = np.broadcast_to(-(128.0 - np.arange(128, dtype=f32))[:, None], (128, 16)).copy()
    dbd8c = np.broadcast_to(np.repeat(-128.0 * np.arange(8, dtype=f32), 16)[None, :], (128, 128)).copy()

    shared = dict(
        wqkvt=wqkvt, wgt=wgt, wut=wut, wdt=wdt, woutt=woutt,
        bqkv=np.asarray(L0_attn_bqkv, f32), sgate=np.asarray(L0_ffn_s_gate, f32),
        sup=np.asarray(L0_ffn_s_up, f32), sdown=np.asarray(L0_ffn_s_down, f32),
        boutv=np.asarray(L0_attn_bout, f32), slopes=np.asarray(L0_attn_slopes, f32),
        ident8=np.eye(128, dtype=np.int8), identf=np.eye(128, dtype=f32),
        d0c=d0c, m01c=m01c, dqc=dqc, dk16c=dk16c, dbd8c=dbd8c,
    )
    in_maps = []
    for c in range(NCORE):
        row = np.zeros(128, dtype=f32)
        zrow = np.zeros(128, dtype=f32)
        for r in range(8):
            if r < c:
                row[r * 16:(r + 1) * 16] = -1024.0 * (c - 1 - r)
                zrow[r * 16:(r + 1) * 16] = 1.0
        m = dict(shared)
        m["xin"] = np.ascontiguousarray(x[0, c * 1024:(c + 1) * 1024, :].T)
        m["dcrwc"] = np.broadcast_to(row[None, :], (128, 128)).copy()
        m["zcc"] = np.broadcast_to(zrow[None, :], (128, 128)).copy()
        in_maps.append(m)
    return in_maps


def kernel(**inputs):
    in_maps = _host_prep(**inputs)
    if "nc" not in _CACHE:
        _CACHE["nc"] = build_program()
    nc = _CACHE["nc"]
    res = run_bass_kernel_spmd(nc, in_maps, core_ids=list(range(NCORE)))
    out = np.empty((1, 8192, 2048), dtype=np.float32)
    for c in range(NCORE):
        out[0, c * 1024:(c + 1) * 1024, :] = np.asarray(res.results[c]["xout"]).T
    return out


# version 06
# speedup vs baseline: 1.0583x; 1.0498x over previous
import numpy as np
from contextlib import ExitStack

import concourse.bass as bass
import concourse.mybir as mybir
from concourse.bass_utils import run_bass_kernel_spmd

F32 = mybir.dt.float32
BF16 = mybir.dt.bfloat16
I8 = mybir.dt.int8
AF = mybir.ActivationFunctionType
ALU = mybir.AluOpType

NCORE = 8
H = 16
C = 2048
KC = 16
T = 1024
NI = 64
RND = 4
IPR = 16
NCH = 8
HHALF = 8


class Eng:

    def __init__(self, name):
        self.name = name
        self.sem = None
        self.n = 0
        self.q = []

    def wait(self, other, val):
        if val is None or val <= 0:
            return
        self.q.append(lambda E, s=other.sem, v=val: E.wait_ge(s, v))

    def do(self, fn, inc=False, incby=1):
        if inc:
            self.n += incby
            self.q.append(lambda E, fn=fn, ib=incby, s=self: fn(E).then_inc(s.sem, ib))
            return self.n
        self.q.append(lambda E, fn=fn: fn(E))
        return None


def build_program(upto="all"):
    nc = bass.Bass("TRN2", target_bir_lowering=False, detect_race_conditions=False)

    def din(name, shape, dty):
        return nc.dram_tensor(name, list(shape), dty, kind="ExternalInput").ap()

    xin = din("xin", [C, T], F32)
    wqkvt = din("wqkvt", [48, 128, 2048], F32)
    wgt = din("wgt", [NI, 128, 2048], I8)
    wut = din("wut", [NI, 128, 2048], I8)
    wdt = din("wdt", [RND, KC, 128, 2048], I8)
    woutt = din("woutt", [2, KC, 128, 1024], F32)
    bqkv = din("bqkv", [6144], F32)
    sgate = din("sgate", [8192], F32)
    sup = din("sup", [8192], F32)
    sdown = din("sdown", [2048], F32)
    boutv = din("boutv", [2048], F32)
    slopes = din("slopes", [16], F32)
    ident8 = din("ident8", [128, 128], I8)
    identf = din("identf", [128, 128], F32)
    d0c = din("d0c", [128, 128], F32)
    m01c = din("m01c", [128, 128], F32)
    dqc = din("dqc", [128, 128], F32)
    dk16c = din("dk16c", [128, 16], F32)
    dbd8c = din("dbd8c", [128, 128], F32)
    dcrwc = din("dcrwc", [128, 128], F32)
    zcc = din("zcc", [128, 128], F32)

    xout = nc.dram_tensor("xout", [C, T], F32, kind="ExternalOutput").ap()
    agin = [nc.dram_tensor(f"agin{i}", [HHALF * 128, 128], F32) for i in range(2)]
    agout = [nc.dram_tensor(f"agout{i}", [NCORE * HHALF * 128, 128], F32) for i in range(2)]

    TEN, VEC, SCA, GPS, SYN = (Eng(n) for n in ("tensor", "vector", "scalar", "gpsimd", "sync"))
    ENGS = dict(tensor=TEN, vector=VEC, scalar=SCA, gpsimd=GPS, sync=SYN)

    with ExitStack() as top:
        for e in ENGS.values():
            e.sem = top.enter_context(nc.semaphore(f"sem_{e.name}"))
        sem_c = [top.enter_context(nc.semaphore(f"sem_c{i}")) for i in range(2)]

        sb = lambda name, shape, dty: top.enter_context(nc.sbuf_tensor(name, list(shape), dty))

        xT = sb("xT", [128, KC * T], F32)
        xnT = sb("xnT", [128, KC * T], BF16)
        wA = [sb(f"wA{i}", [128, 2048], BF16) for i in range(2)]
        wB = [sb(f"wB{i}", [128, 2048], BF16) for i in range(2)]
        wC = [sb(f"wC{i}", [128, 2048], BF16) for i in range(2)]
        identb_s = sb("identb_s", [128, 128], BF16)
        masks = sb("masks", [128, H * 128], BF16)
        qdec = sb("qdec", [128, H * 128], BF16)
        kdecs = sb("kdecs", [128, 16], F32)
        bds = sb("bds", [128, 16], F32)
        bdpow = sb("bdpow", [128, 128], F32)
        rw = sb("rw", [128, 128], F32)
        biasA = sb("biasA", [128, 128], F32)
        biasB = sb("biasB", [128, 80], F32)
        sdh = sb("sdh", [128, 16], F32)
        slope_bc = sb("slope_bc", [128, 16], F32)
        ones_b = sb("ones_b", [128, 1], BF16)
        ones_f = sb("ones_f", [1, 128], F32)
        eps_t = sb("eps_t", [128, 1], F32)
        rowA = sb("rowA", [1, T], F32)
        rowB = sb("rowB", [1, T], F32)
        rowC = sb("rowC", [1, T], F32)
        row_m, row_mn, row_msq = rowA, rowB, rowC
        row_var, row_std, row_rstd = rowA, rowC, rowA

        rel = {}

        def acq(eng, slot):
            r = rel.get(slot)
            if r is not None:
                eng.wait(r[0], r[1])

        def release(slot, eng, val):
            rel[slot] = (eng, val)

        def flush():
            for name, e in ENGS.items():
                fns = e.q
                e.q = []
                if fns:
                    getattr(block, name)(lambda E, fns=fns: [f(E) for f in fns])

        def gdma(out, in_, slot=None):
            if slot is not None:
                acq(GPS, slot)
            return GPS.do(lambda E, o=out, i=in_: E.dma_start(out=o, in_=i), inc=True, incby=16)

        def ydma(out, in_):
            return SYN.do(lambda E, o=out, i=in_: E.dma_start(out=o, in_=i), inc=True, incby=16)

        block = top.enter_context(nc.Block())

        identf_s = sb("identf_s", [128, 128], F32)
        d0_s = sb("d0_s", [128, 128], F32)
        m01_s = sb("m01_s", [128, 128], F32)
        dq_s = sb("dq_s", [128, 128], F32)
        dk16_s = sb("dk16_s", [128, 16], F32)
        dbd8_s = sb("dbd8_s", [128, 128], F32)
        dcrw_s = sb("dcrw_s", [128, 128], F32)
        zc_s = sb("zc_s", [128, 128], F32)
        slope_rh = sb("slope_rh", [128, 128], F32)
        slope_h8 = sb("slope_h8", [128, 128], F32)
        stage1 = sb("stage1", [128, 128], F32)
        stage2 = sb("stage2", [128, 128], F32)
        mexp = [sb(f"mexp{i}", [128, 128], F32) for i in range(2)]

        with ExitStack() as sctx:
            pst1 = sctx.enter_context(nc.psum_tensor("pst1", [128, 512], F32))
            pst2 = sctx.enter_context(nc.psum_tensor("pst2", [128, 512], F32))

            v_xload_k = []
            for k in range(KC):
                v_xload_k.append(ydma(xT[:, k * T:(k + 1) * T], xin[k * 128:(k + 1) * 128, :]))
            v_xload = v_xload_k[-1]
            for dst, src in ((identf_s, identf), (d0_s, d0c), (m01_s, m01c), (dq_s, dqc),
                             (dk16_s, dk16c), (dbd8_s, dbd8c), (dcrw_s, dcrwc), (zc_s, zcc)):
                ydma(dst[:, :], src)
            ydma(slope_bc[:, :], slopes[None, :].broadcast_to([128, 16]))
            ydma(slope_rh[:, :].rearrange("p (r h) -> p r h", r=8),
                 slopes[None, None, :].broadcast_to([128, 8, 16]))
            ydma(slope_h8[:, :].rearrange("p (t h) -> p t h", t=8),
                 slopes[None, None, :].broadcast_to([128, 8, 16]))
            ydma(stage1[0:48, :], bqkv.rearrange("(a b) -> a b", b=128))
            ydma(stage1[48:112, :], sgate.rearrange("(a b) -> a b", b=128))
            ydma(stage1[112:128, :], sdown.rearrange("(a b) -> a b", b=128))
            ydma(stage2[0:64, :], sup.rearrange("(a b) -> a b", b=128))
            v_sload = ydma(stage2[64:80, :], boutv.rearrange("(a b) -> a b", b=128))
            gdma(identb_s[:, :], ident8)

            TEN.wait(SYN, v_sload)
            t_tr1 = TEN.do(lambda E: E.transpose(pst1[:, 0:128], stage1[:, :], identf_s[:, :]), inc=True)
            t_tr2 = TEN.do(lambda E: E.transpose(pst2[:, 0:128], stage2[:, :], identf_s[:, :]), inc=True)
            VEC.wait(TEN, t_tr1)
            VEC.do(lambda E: E.tensor_copy(biasA[:, :], pst1[:, 0:128]))
            VEC.wait(TEN, t_tr2)
            v_bB = VEC.do(lambda E: E.tensor_copy(biasB[:, :], pst2[:, 0:80]), inc=True)
            VEC.wait(VEC, v_bB)
            VEC.do(lambda E: E.tensor_scalar_mul(sdh[:, :], biasA[:, 112:128], 0.5))
            VEC.do(lambda E: E.memset(ones_b[:, :], 1.0))
            VEC.do(lambda E: E.memset(ones_f[:, :], 1.0))
            VEC.do(lambda E: E.memset(eps_t[:, :], 1e-5))

            flush()

        def emit_decay_tables():
            SCA.wait(SYN, v_sload)
            s_bds = SCA.do(lambda E: E.activation(bds[:, :], slope_bc[:, :], AF.Exp, scale=-128.0), inc=True)
            VEC.wait(SYN, v_sload)
            VEC.do(lambda E: E.tensor_mul(mexp[0][:, 0:16], dk16_s[:, :], slope_bc[:, :]))
            v_kd = VEC.do(lambda E: E.tensor_mul(mexp[1][:, :], dbd8_s[:, :], slope_h8[:, :]), inc=True)
            SCA.wait(VEC, v_kd)
            SCA.do(lambda E: E.activation(kdecs[:, :], mexp[0][:, 0:16], AF.Exp))
            s_bdp = SCA.do(lambda E: E.activation(bdpow[:, :], mexp[1][:, :], AF.Exp), inc=True)
            VEC.wait(SCA, s_bdp)
            v_rw1 = VEC.do(lambda E: E.tensor_mul(mexp[0][:, :], dcrw_s[:, :], slope_rh[:, :]), inc=True)
            SCA.wait(VEC, v_rw1)
            SCA.wait(SCA, s_bdp)
            s_rw = SCA.do(lambda E: E.activation(mexp[1][:, :], mexp[0][:, :], AF.Exp), inc=True)
            VEC.wait(SCA, s_rw)
            VEC.do(lambda E: E.tensor_mul(rw[:, :], mexp[1][:, :], zc_s[:, :]))
            for h in range(H):
                sc = slope_bc[:, h:h + 1]
                s_m = SCA.do(lambda E, sc=sc, h=h: E.activation(
                    mexp[h % 2][:, :], d0_s[:, :], AF.Exp, scale=sc), inc=True)
                VEC.wait(SCA, s_m)
                v_m = VEC.do(lambda E, h=h: E.tensor_mul(
                    masks[:, h * 128:(h + 1) * 128], mexp[h % 2][:, :], m01_s[:, :]), inc=True)
                SCA.wait(VEC, v_m)
                SCA.do(lambda E, sc=sc, h=h: E.activation(
                    qdec[:, h * 128:(h + 1) * 128], dq_s[:, :], AF.Exp, scale=sc))
            flush()

        def layer_norm(tag, xready_eng, xready_val, sub="all", xready_k=None, ten_guard=0):
            with ExitStack() as ctx:
                ps1 = ctx.enter_context(nc.psum_tensor(f"ps1{tag}", [128, T], F32))
                ps2 = ctx.enter_context(nc.psum_tensor(f"ps2{tag}", [128, T], F32))
                psa = ctx.enter_context(nc.psum_tensor(f"psa{tag}", [128, T], F32))
                psb = ctx.enter_context(nc.psum_tensor(f"psb{tag}", [128, T], F32))
                xb2 = [ctx.enter_context(nc.sbuf_tensor(f"xb{tag}{i}", [128, T], BF16)) for i in range(2)]
                sq2 = [ctx.enter_context(nc.sbuf_tensor(f"sq{tag}{i}", [128, T], BF16)) for i in range(2)]
                mm_done = [0] * KC
                for k in range(KC):
                    if k == 0 and ten_guard:
                        SCA.wait(TEN, ten_guard)
                    SCA.wait(xready_eng, xready_k[k] if xready_k else (xready_val if k == 0 else 0))
                    acq(SCA, f"xb{k % 2}")
                    xb_done = SCA.do(lambda E, k=k: E.activation(
                        xb2[k % 2][:, :], xT[:, k * T:(k + 1) * T], AF.Copy), inc=True)
                    VEC.wait(SCA, xb_done)
                    acq(VEC, f"sq{k % 2}")
                    sq_done = VEC.do(lambda E, k=k: E.tensor_mul(
                        sq2[k % 2][:, :], xb2[k % 2][:, :], xb2[k % 2][:, :]), inc=True)
                    TEN.wait(SCA, xb_done)
                    TEN.wait(VEC, sq_done)
                    if k == 0:
                        acq(TEN, "ln_psum")
                    st, sp = (k == 0), (k == KC - 1)
                    for half in range(2):
                        TEN.do(lambda E, k=k, half=half, st=st, sp=sp: E.matmul(
                            ps1[0:1, half * 512:(half + 1) * 512], ones_b[:, :],
                            xb2[k % 2][:, half * 512:(half + 1) * 512], start=st, stop=sp))
                    TEN.do(lambda E, k=k, st=st, sp=sp: E.matmul(
                        ps2[0:1, 0:512], ones_b[:, :], sq2[k % 2][:, 0:512], start=st, stop=sp))
                    mm_done[k] = TEN.do(lambda E, k=k, st=st, sp=sp: E.matmul(
                        ps2[0:1, 512:1024], ones_b[:, :], sq2[k % 2][:, 512:1024],
                        start=st, stop=sp), inc=True)
                    release(f"xb{k % 2}", TEN, mm_done[k])
                    release(f"sq{k % 2}", TEN, mm_done[k])
                VEC.wait(TEN, mm_done[KC - 1])
                v_r1 = VEC.do(lambda E: E.tensor_scalar_mul(row_m[:, :], ps1[0:1, :], 1.0 / C), inc=True)
                VEC.wait(VEC, v_r1)
                VEC.do(lambda E: E.tensor_scalar_mul(row_mn[:, :], row_m[:, :], -1.0))
                v_r2 = VEC.do(lambda E: E.tensor_mul(row_msq[:, :], row_m[:, :], row_m[:, :]), inc=True)
                VEC.wait(VEC, v_r2)
                v_r3 = VEC.do(lambda E: E.tensor_scalar_mul(row_msq[:, :], row_msq[:, :], -1.0), inc=True)
                VEC.wait(VEC, v_r3)
                v_var = VEC.do(lambda E: E.scalar_tensor_tensor(
                    out=row_var[:, :], in0=ps2[0:1, :], scalar=1.0 / C, in1=row_msq[:, :],
                    op0=ALU.mult, op1=ALU.add), inc=True)
                SCA.wait(VEC, v_var)
                s_std = SCA.do(lambda E: E.activation(row_std[:, :], row_var[:, :], AF.Sqrt, bias=eps_t[0:1, 0:1]), inc=True)
                VEC.wait(SCA, s_std)
                v_rstd = VEC.do(lambda E: E.reciprocal(row_rstd[:, :], row_std[:, :]), inc=True)
                TEN.wait(VEC, v_rstd)
                for half in range(2):
                    TEN.do(lambda E, half=half: E.matmul(
                        psa[:, half * 512:(half + 1) * 512], ones_f[:, :],
                        row_rstd[0:1, half * 512:(half + 1) * 512], start=True, stop=True))
                TEN.do(lambda E: E.matmul(
                    psb[:, 0:512], ones_f[:, :], row_mn[0:1, 0:512], start=True, stop=True))
                t_bc = TEN.do(lambda E: E.matmul(
                    psb[:, 512:1024], ones_f[:, :], row_mn[0:1, 512:1024],
                    start=True, stop=True), inc=True)
                VEC.wait(TEN, t_bc)
                v_xn = 0
                v_xnk = []
                for k in range(KC):
                    v_n1 = VEC.do(lambda E, k=k: E.tensor_tensor(
                        xnT[:, k * T:(k + 1) * T], xT[:, k * T:(k + 1) * T], psb[:, :], ALU.add),
                        inc=True)
                    VEC.wait(VEC, v_n1)
                    v_xn = VEC.do(lambda E, k=k: E.tensor_tensor(
                        xnT[:, k * T:(k + 1) * T], xnT[:, k * T:(k + 1) * T], psa[:, :], ALU.mult),
                        inc=True)
                    v_xnk.append(v_xn)
                release("ln_psum", VEC, v_xn)
                flush()
                return v_xn, v_xnk

        def ffn_pass(pass_idx, v_xnk):
            mg_list = [0] * KC
            with ExitStack() as ctx:
                fuse = ctx.enter_context(nc.sbuf_tensor(f"fuse{pass_idx}", [128, IPR * T], BF16))
                gtmp = [ctx.enter_context(nc.sbuf_tensor(f"gt{pass_idx}_{i}", [128, T], BF16)) for i in range(2)]
                utmp = [ctx.enter_context(nc.sbuf_tensor(f"ut{pass_idx}_{i}", [128, T], BF16)) for i in range(2)]
                pg = [ctx.enter_context(nc.psum_tensor(f"pg{pass_idx}_{i}", [128, T], F32)) for i in range(2)]
                pu = [ctx.enter_context(nc.psum_tensor(f"pu{pass_idx}_{i}", [128, T], F32)) for i in range(2)]
                psy = pg + pu
                pskey = ["pg0", "pg1", "pu0", "pu1"]
                for r in range(RND):
                    v_fuse_round = 0
                    for ii in range(IPR):
                        gi = r * IPR + ii
                        nwg = gdma(wA[gi % 2][:, :], wgt[gi], slot=f"wA{gi % 2}")
                        nwu = gdma(wB[gi % 2][:, :], wut[gi], slot=f"wB{gi % 2}")
                        TEN.wait(GPS, nwg)
                        acq(TEN, f"pg{gi % 2}")
                        t_g = 0
                        for k in range(KC):
                            if gi == 0:
                                TEN.wait(VEC, v_xnk[k])
                            for half in range(2):
                                last = (k == KC - 1 and half == 1)
                                t_g = TEN.do(lambda E, gi=gi, half=half, k=k: E.matmul(
                                    pg[gi % 2][:, half * 512:(half + 1) * 512],
                                    wA[gi % 2][:, k * 128:(k + 1) * 128],
                                    xnT[:, k * T + half * 512: k * T + (half + 1) * 512],
                                    start=(k == 0), stop=(k == KC - 1)), inc=last)
                        release(f"wA{gi % 2}", TEN, t_g)
                        TEN.wait(GPS, nwu)
                        acq(TEN, f"pu{gi % 2}")
                        t_u = 0
                        for k in range(KC):
                            for half in range(2):
                                last = (k == KC - 1 and half == 1)
                                t_u = TEN.do(lambda E, gi=gi, half=half, k=k: E.matmul(
                                    pu[gi % 2][:, half * 512:(half + 1) * 512],
                                    wB[gi % 2][:, k * 128:(k + 1) * 128],
                                    xnT[:, k * T + half * 512: k * T + (half + 1) * 512],
                                    start=(k == 0), stop=(k == KC - 1)), inc=last)
                        release(f"wB{gi % 2}", TEN, t_u)
                        SCA.wait(TEN, t_g)
                        acq(SCA, f"gtmp{gi % 2}")
                        SCA.do(lambda E, gi=gi: E.activation(
                            gtmp[gi % 2][:, :], pg[gi % 2][:, :], AF.Silu,
                            scale=biasA[:, 48 + gi % 64:49 + gi % 64]))
                        SCA.wait(TEN, t_u)
                        acq(SCA, f"utmp{gi % 2}")
                        s_e = SCA.do(lambda E, gi=gi: E.activation(
                            utmp[gi % 2][:, :], pu[gi % 2][:, :], AF.Copy,
                            scale=biasB[:, gi % 64:gi % 64 + 1]), inc=True)
                        release(f"pg{gi % 2}", SCA, s_e)
                        release(f"pu{gi % 2}", SCA, s_e)
                        VEC.wait(SCA, s_e)
                        acq(VEC, f"fuse{ii}")
                        v_f = VEC.do(lambda E, gi=gi, ii=ii: E.tensor_mul(
                            fuse[:, ii * T:(ii + 1) * T], gtmp[gi % 2][:, :], utmp[gi % 2][:, :]), inc=True)
                        release(f"gtmp{gi % 2}", VEC, v_f)
                        release(f"utmp{gi % 2}", VEC, v_f)
                        if ii == IPR - 1:
                            v_fuse_round = v_f
                    t_y_last = 0
                    for m in range(KC):
                        nwd = gdma(wC[m % 2][:, :], wdt[r, m], slot=f"wC{m % 2}")
                        TEN.wait(VEC, v_fuse_round)
                        TEN.wait(GPS, nwd)
                        acq(TEN, pskey[m % 4])
                        t_y = 0
                        for ii in range(IPR):
                            for half in range(2):
                                last = (ii == IPR - 1 and half == 1)
                                t_y = TEN.do(lambda E, m=m, half=half, ii=ii: E.matmul(
                                    psy[m % 4][:, half * 512:(half + 1) * 512],
                                    wC[m % 2][:, ii * 128:(ii + 1) * 128],
                                    fuse[:, ii * T + half * 512: ii * T + (half + 1) * 512],
                                    start=(ii == 0), stop=(ii == IPR - 1)), inc=last)
                        release(f"wC{m % 2}", TEN, t_y)
                        t_y_last = t_y
                        VEC.wait(TEN, t_y)
                        v_mg = VEC.do(lambda E, m=m: E.scalar_tensor_tensor(
                            out=xT[:, m * T:(m + 1) * T], in0=psy[m % 4][:, :],
                            scalar=sdh[:, m:m + 1], in1=xT[:, m * T:(m + 1) * T],
                            op0=ALU.mult, op1=ALU.add), inc=True)
                        release(pskey[m % 4], VEC, v_mg)
                        if r == RND - 1:
                            mg_list[m] = v_mg
                        if pass_idx == 1 and r == RND - 1:
                            SYN.wait(VEC, v_mg)
                            ydma(xout[m * 128:(m + 1) * 128, :], xT[:, m * T:(m + 1) * T])
                    for ii in range(IPR):
                        release(f"fuse{ii}", TEN, t_y_last)
                v_last_merge = VEC.n
                release("ln_psum", VEC, v_last_merge)
                flush()
                return v_last_merge, (mg_list, t_y_last)

        def attn_half(half_idx, v_xn, v_xnk, o_sb, qhat, hb, sbufs, prev_done):
            qTh2, kTh2, vtok2, vT_scr = hb
            scb, qt_scr, kt_scr, Sb, Sf = sbufs
            h0 = half_idx * HHALF
            with ExitStack() as ctx:
                pJ0 = ctx.enter_context(nc.psum_tensor(f"pJ0{half_idx}", [128, T], F32))
                pJ1 = ctx.enter_context(nc.psum_tensor(f"pJ1{half_idx}", [128, T], F32))
                pQ, pK, pV = pJ0, pJ1, pJ0
                pscS_t = ctx.enter_context(nc.psum_tensor(f"pscS{half_idx}", [128, 512], F32))
                po_t = ctx.enter_context(nc.psum_tensor(f"po{half_idx}", [128, 512], F32))
                psc, pS, po = pscS_t[:, 0:128], pscS_t[:, 256:384], po_t[:, 0:128]
                pbkA = ctx.enter_context(nc.psum_tensor(f"pbkA{half_idx}", [128, 1024], BF16))
                pbkB = ctx.enter_context(nc.psum_tensor(f"pbkB{half_idx}", [128, 1024], BF16))
                pktr = pbkA[:, 0:128]
                ptrv = [pbkA[:, 128:256], pbkB[:, 0:128]]

                st = {}

                def qkv_dmas(hh):
                    h = h0 + hh
                    st[hh] = dict(
                        nq=gdma(wB[hh % 2][:, :], wqkvt[h], slot=f"wB{hh % 2}"),
                        nk=gdma(wC[hh % 2][:, :], wqkvt[16 + h], slot=f"wC{hh % 2}"),
                        nv=gdma(wA[hh % 2][:, :], wqkvt[32 + h], slot=f"wA{hh % 2}"))

                def mm_piece(hh, which, part, stagger=False):
                    d = st[hh]
                    ps, wbuf, wslot = {
                        "q": (pQ, wB, f"wB{hh % 2}"),
                        "k": (pK, wC, f"wC{hh % 2}"),
                        "v": (pV, wA, f"wA{hh % 2}"),
                    }[which]
                    if part == 0:
                        TEN.wait(GPS, d["n" + which])
                        if which == "q":
                            acq(TEN, f"pQ")
                        elif which == "k":
                            acq(TEN, f"pK")
                        else:
                            TEN.wait(SCA, d["s_evq"])
                    t_last = 0
                    for k in range(part * 8, part * 8 + 8):
                        if stagger and v_xnk:
                            TEN.wait(VEC, v_xnk[k])
                        for half in range(2):
                            t_last = TEN.do(lambda E, hh=hh, half=half, k=k, ps=ps, wbuf=wbuf: E.matmul(
                                ps[:, half * 512:(half + 1) * 512],
                                wbuf[hh % 2][:, k * 128:(k + 1) * 128],
                                xnT[:, k * T + half * 512: k * T + (half + 1) * 512],
                                start=(k == 0), stop=(k == KC - 1)),
                                inc=(k == KC - 1 and half == 1))
                    if part == 1:
                        st[hh]["t_" + which] = t_last
                        release(wslot, TEN, t_last)

                def evac(hh, which):
                    d = st[hh]
                    h = h0 + hh
                    if which == "q":
                        SCA.wait(TEN, d["t_q"])
                        acq(SCA, f"qTh{hh % 2}")
                        d["s_evq"] = SCA.do(lambda E, h=h, hh=hh: E.activation(
                            qTh2[hh % 2][:, :], pQ[:, :], AF.Identity, bias=biasA[:, h:h + 1]), inc=True)
                    elif which == "k":
                        SCA.wait(TEN, d["t_k"])
                        acq(SCA, f"kTh{hh % 2}")
                        d["s_evk"] = SCA.do(lambda E, h=h, hh=hh: E.activation(
                            kTh2[hh % 2][:, :], pK[:, :], AF.Identity, bias=biasA[:, 16 + h:17 + h]), inc=True)
                        release("pK", SCA, d["s_evk"])
                    else:
                        SCA.wait(TEN, d["t_v"])
                        acq(SCA, "vT_scr")
                        d["s_evv"] = SCA.do(lambda E, h=h: E.activation(
                            vT_scr[:, :], pV[:, :], AF.Identity, bias=biasA[:, 32 + h:33 + h]), inc=True)
                        release("pQ", SCA, d["s_evv"])

                def vtr_piece(hh, ts):
                    d = st[hh]
                    for t in ts:
                        if t == 0:
                            TEN.wait(SCA, d["s_evv"])
                        bk = "bkA" if t % 2 == 0 else "bkB"
                        acq(TEN, bk)
                        t_tr = TEN.do(lambda E, t=t: E.transpose(
                            ptrv[t % 2], vT_scr[:, t * 128:(t + 1) * 128], identb_s[:, :]), inc=True)
                        VEC.wait(TEN, t_tr)
                        if t == 0:
                            acq(VEC, f"vtokb{hh % 2}")
                        v_ev = VEC.do(lambda E, t=t, hh=hh: E.tensor_copy(
                            vtok2[hh % 2][:, t * 128:(t + 1) * 128], ptrv[t % 2]), inc=True)
                        release(bk, VEC, v_ev)
                        d["v_vtok"] = v_ev
                        d.setdefault("trs", {})[t] = t_tr
                    if ts and ts[-1] == NCH - 1:
                        release("vT_scr", TEN, d["trs"][NCH - 1])

                def scan_step(hh, t):
                    d = st[hh]
                    h = h0 + hh
                    qTh, kTh, vtok = qTh2[hh % 2], kTh2[hh % 2], vtok2[hh % 2]
                    if t == 0:
                        TEN.wait(VEC, d["v_vtok"])
                        TEN.wait(SCA, d["s_evk"])
                    acq(TEN, "bkA")
                    t_kt = TEN.do(lambda E, t=t, kTh=kTh: E.transpose(
                        pktr, kTh[:, t * 128:(t + 1) * 128], identb_s[:, :]), inc=True)
                    d["t_kt"] = t_kt
                    acq(TEN, "psc")
                    t_sc = TEN.do(lambda E, t=t, kTh=kTh, qTh=qTh: E.matmul(
                        psc, kTh[:, t * 128:(t + 1) * 128], qTh[:, t * 128:(t + 1) * 128],
                        start=True, stop=True), inc=True)
                    SCA.wait(TEN, t_kt)
                    acq(SCA, "kt_scr")
                    s_kt = SCA.do(lambda E, h=h: E.activation(
                        kt_scr[:, :], pktr, AF.Copy, scale=kdecs[:, h:h + 1]), inc=True)
                    release("bkA", SCA, s_kt)
                    acq(VEC, "qt_scr")
                    v_q0 = VEC.do(lambda E, h=h, t=t, qTh=qTh: E.tensor_mul(
                        qt_scr[:, :], qTh[:, t * 128:(t + 1) * 128], qdec[:, h * 128:(h + 1) * 128]),
                        inc=True)
                    VEC.wait(TEN, t_sc)
                    acq(VEC, "scb")
                    v_sc = VEC.do(lambda E, h=h: E.tensor_mul(
                        scb[:, :], psc, masks[:, h * 128:(h + 1) * 128]), inc=True)
                    VEC.wait(VEC, v_q0)
                    v_qt = VEC.do(lambda E, h=h, hh=hh, t=t: E.tensor_scalar_mul(
                        qhat[:, hh * T + t * 128: hh * T + (t + 1) * 128], qt_scr[:, :],
                        bdpow[:, t * 16 + h: t * 16 + h + 1]), inc=True)
                    d["v_qt"] = v_qt
                    TEN.wait(VEC, v_qt)
                    acq(TEN, "po")
                    if t == 0:
                        t_o = TEN.do(lambda E, t=t, vtok=vtok: E.matmul(
                            po, vtok[:, t * 128:(t + 1) * 128], scb[:, :],
                            start=True, stop=True), inc=True)
                    else:
                        TEN.do(lambda E, t=t, vtok=vtok: E.matmul(
                            po, vtok[:, t * 128:(t + 1) * 128], scb[:, :],
                            start=True, stop=False))
                        t_o = TEN.do(lambda E: E.matmul(
                            po, Sb[:, :], qt_scr[:, :], start=False, stop=True), inc=True)
                    release("scb", TEN, t_o)
                    release("qt_scr", TEN, t_o)
                    TEN.wait(SCA, s_kt)
                    acq(TEN, "pS")
                    t_S = TEN.do(lambda E, t=t, vtok=vtok: E.matmul(
                        pS, kt_scr[:, :], vtok[:, t * 128:(t + 1) * 128],
                        start=True, stop=True), inc=True)
                    release("kt_scr", TEN, t_S)
                    d["t_S"] = t_S
                    VEC.wait(TEN, t_S)
                    v_su = VEC.do(lambda E, h=h, hh=hh: E.scalar_tensor_tensor(
                        out=Sf[hh % 2][:, :], in0=Sf[hh % 2][:, :], scalar=bds[:, h:h + 1],
                        in1=pS, op0=ALU.mult, op1=ALU.add), inc=True)
                    release("pS", VEC, v_su)
                    release("psc", VEC, v_su)
                    d["v_su"] = v_su
                    if t < NCH - 1:
                        VEC.wait(TEN, t_o)
                        VEC.wait(VEC, v_su)
                        VEC.do(lambda E, hh=hh: E.tensor_copy(Sb[:, :], Sf[hh % 2][:, :]))
                    SCA.wait(TEN, t_o)
                    s_oev = SCA.do(lambda E, hh=hh, t=t: E.activation(
                        o_sb[:, hh * T + t * 128: hh * T + (t + 1) * 128], po, AF.Copy), inc=True)
                    release("po", SCA, s_oev)

                qkv_dmas(0)
                if prev_done:
                    TEN.wait(VEC, prev_done)
                TEN.wait(VEC, v_xn)
                mm_piece(0, "q", 0, stagger=(half_idx == 0))
                mm_piece(0, "q", 1, stagger=(half_idx == 0))
                mm_piece(0, "k", 0)
                mm_piece(0, "k", 1)
                evac(0, "q")
                evac(0, "k")
                mm_piece(0, "v", 0)
                mm_piece(0, "v", 1)
                evac(0, "v")
                vtr_piece(0, [0, 1, 2, 3])
                vtr_piece(0, [4, 5, 6, 7])
                n_ast = 0
                for hh in range(HHALF):
                    nxt = hh + 1
                    has_next = nxt < HHALF
                    if has_next:
                        qkv_dmas(nxt)
                    acq(VEC, f"Sf{hh % 2}")
                    acq(VEC, f"Sfv{hh % 2}")
                    VEC.do(lambda E, hh=hh: E.memset(Sf[hh % 2][:, :], 0.0))
                    for t in range(NCH):
                        scan_step(hh, t)
                        if has_next:
                            if t == 0:
                                mm_piece(nxt, "q", 0)
                            elif t == 1:
                                mm_piece(nxt, "q", 1)
                            elif t == 2:
                                mm_piece(nxt, "k", 0)
                                evac(nxt, "q")
                            elif t == 3:
                                mm_piece(nxt, "k", 1)
                                evac(nxt, "k")
                            elif t == 4:
                                mm_piece(nxt, "v", 0)
                            elif t == 5:
                                mm_piece(nxt, "v", 1)
                                evac(nxt, "v")
                            elif t == 6:
                                vtr_piece(nxt, [0, 1, 2, 3])
                            else:
                                vtr_piece(nxt, [4, 5, 6, 7])
                    d = st[hh]
                    SYN.wait(VEC, d["v_su"])
                    n_ast = ydma(agin[half_idx].ap()[hh * 128:(hh + 1) * 128, :], Sf[hh % 2][:, :])
                    release(f"Sf{hh % 2}", SYN, n_ast)
                    release(f"Sfv{hh % 2}", VEC, d["v_su"])
                    release(f"qTh{hh % 2}", VEC, d["v_qt"])
                    release(f"kTh{hh % 2}", TEN, d["t_kt"])
                    release(f"vtokb{hh % 2}", TEN, d["t_S"])
                GPS.wait(SYN, n_ast)
                GPS.do(lambda E, hi=half_idx, cs=sem_c[half_idx]: E.collective_compute(
                    "AllGather", ALU.bypass, replica_groups=[list(range(NCORE))],
                    ins=[agin[hi].ap().opt()], outs=[agout[hi].ap().opt()]).then_inc(cs, 1))
                scan_end = dict(t=TEN.n, v=VEC.n, s=SCA.n)
                flush()
                return scan_end

        def attn_global(half_idx, o_sb, qhat, scan_end):
            h0 = half_idx * HHALF
            with ExitStack() as ctx:
                agl2 = [ctx.enter_context(nc.sbuf_tensor(f"agl{half_idx}_{i}", [128, 512], F32)) for i in range(2)]
                S_in = ctx.enter_context(nc.sbuf_tensor(f"S_in{half_idx}", [128, HHALF * 128], F32))
                S_in_b = ctx.enter_context(nc.sbuf_tensor(f"S_inb{half_idx}", [128, HHALF * 128], BF16))
                pog = [ctx.enter_context(nc.psum_tensor(f"pog{half_idx}_{i}", [128, T], F32)) for i in range(2)]
                pgo = [ctx.enter_context(nc.psum_tensor(f"pgo{half_idx}_{i}", [128, T], F32)) for i in range(2)]
                SYN.q.append(lambda E, cs=sem_c[half_idx]: E.wait_ge(cs, 1))
                SYN.wait(TEN, scan_end["t"])
                SYN.wait(VEC, scan_end["v"])
                SYN.wait(SCA, scan_end["s"])
                v_cmb = 0
                for rg in range(NCORE * 2):
                    r, grp = rg // 2, rg % 2
                    acq(SYN, f"agl{rg % 2}")
                    n_l = ydma(agl2[rg % 2][:, :].rearrange("d (h e) -> d h e", h=4),
                               agout[half_idx].ap()[r * HHALF * 128 + grp * 512:
                                                    r * HHALF * 128 + (grp + 1) * 512, :]
                               .rearrange("(h d) e -> d h e", h=4))
                    VEC.wait(SYN, n_l)
                    for hl in range(4):
                        hh = grp * 4 + hl
                        h = h0 + hh
                        if r == 0:
                            if hh == 0:
                                acq(VEC, "S_in_cast")
                            v_cmb = VEC.do(lambda E, hl=hl, hh=hh, h=h, rg=rg, r=r: E.tensor_scalar_mul(
                                S_in[:, hh * 128:(hh + 1) * 128],
                                agl2[rg % 2][:, hl * 128:(hl + 1) * 128],
                                rw[:, r * 16 + h: r * 16 + h + 1]), inc=(hl == 3))
                        else:
                            v_cmb = VEC.do(lambda E, hl=hl, hh=hh, h=h, rg=rg, r=r: E.scalar_tensor_tensor(
                                out=S_in[:, hh * 128:(hh + 1) * 128],
                                in0=agl2[rg % 2][:, hl * 128:(hl + 1) * 128],
                                scalar=rw[:, r * 16 + h: r * 16 + h + 1],
                                in1=S_in[:, hh * 128:(hh + 1) * 128],
                                op0=ALU.mult, op1=ALU.add), inc=(hl == 3))
                    release(f"agl{rg % 2}", VEC, v_cmb)
                SCA.wait(VEC, v_cmb)
                acq(SCA, "S_in_b")
                s_sib = SCA.do(lambda E: E.activation(S_in_b[:, :], S_in[:, :], AF.Copy), inc=True)
                release("S_in_cast", SCA, s_sib)
                t_og = 0
                v_oadd = 0
                for hh in range(HHALF):
                    if hh == 0:
                        TEN.wait(SCA, s_sib)
                        VEC.wait(SCA, scan_end["s"])
                    acq(TEN, f"pog{hh % 2}")
                    for half in range(2):
                        t_og = TEN.do(lambda E, hh=hh, half=half: E.matmul(
                            pog[hh % 2][:, half * 512:(half + 1) * 512],
                            S_in_b[:, hh * 128:(hh + 1) * 128],
                            qhat[:, hh * T + half * 512: hh * T + (half + 1) * 512],
                            start=True, stop=True), inc=(half == 1))
                    VEC.wait(TEN, t_og)
                    v_oadd = VEC.do(lambda E, hh=hh: E.tensor_tensor(
                        o_sb[:, hh * T:(hh + 1) * T], o_sb[:, hh * T:(hh + 1) * T],
                        pog[hh % 2][:, :], ALU.add), inc=True)
                    release(f"pog{hh % 2}", VEC, v_oadd)
                release("S_in_b", TEN, t_og)
                v_m2 = 0
                v_m2_list = [0] * KC
                for m in range(KC):
                    nw = gdma(wA[m % 2][:, 0:1024], woutt[half_idx, m], slot=f"wA{m % 2}")
                    if m == 0:
                        TEN.wait(VEC, v_oadd)
                    TEN.wait(GPS, nw)
                    acq(TEN, f"pgo{m % 2}")
                    t_po = 0
                    for hh in range(HHALF):
                        for half in range(2):
                            last = (hh == HHALF - 1 and half == 1)
                            t_po = TEN.do(lambda E, m=m, half=half, hh=hh: E.matmul(
                                pgo[m % 2][:, half * 512:(half + 1) * 512],
                                wA[m % 2][:, hh * 128:(hh + 1) * 128],
                                o_sb[:, hh * T + half * 512: hh * T + (half + 1) * 512],
                                start=(hh == 0), stop=(hh == HHALF - 1)), inc=last)
                    release(f"wA{m % 2}", TEN, t_po)
                    VEC.wait(TEN, t_po)
                    if half_idx == 0:
                        v_m2 = VEC.do(lambda E, m=m: E.scalar_tensor_tensor(
                            out=xT[:, m * T:(m + 1) * T], in0=pgo[m % 2][:, :],
                            scalar=biasB[:, 64 + m:65 + m], in1=xT[:, m * T:(m + 1) * T],
                            op0=ALU.add, op1=ALU.add), inc=True)
                    else:
                        v_m2 = VEC.do(lambda E, m=m: E.tensor_tensor(
                            xT[:, m * T:(m + 1) * T], xT[:, m * T:(m + 1) * T],
                            pgo[m % 2][:, :], ALU.add), inc=True)
                    release(f"pgo{m % 2}", VEC, v_m2)
                    v_m2_list[m] = v_m2
                release("ln_psum", VEC, v_m2)
                flush()
                return v_m2, (v_m2_list, t_po)

        def _bail(ready_eng, ready_val):
            SYN.wait(ready_eng, ready_val)
            for m in range(KC):
                ydma(xout[m * 128:(m + 1) * 128, :], xT[:, m * T:(m + 1) * T])
            SYN.q.append(lambda E, s=SYN, v=SYN.n: E.wait_ge(s.sem, v))
            flush()

        if upto == "setup":
            _bail(SYN, v_xload)
            return nc
        v_xn1, v_xnk1 = layer_norm("a", SYN, v_xload, xready_k=v_xload_k)
        emit_decay_tables()
        if upto == "ln1":
            _bail(VEC, v_xn1)
            return nc
        v_ffn1, (mg1, ty1) = ffn_pass(0, v_xnk1)
        if upto == "ffn1":
            _bail(VEC, v_ffn1)
            return nc
        if upto.startswith("ffn1c"):
            rg = [list(range(NCORE))] if upto == "ffn1c" else [[0, 1], [2, 3], [4, 5], [6, 7]]
            GPS.wait(SYN, v_xload)
            GPS.do(lambda E, rg=rg: E.collective_compute(
                "AllGather", ALU.bypass, replica_groups=rg,
                ins=[agin[0].ap()[0:128, :].opt()], outs=[agout[0].ap()[0:len(rg[0]) * 128, :].opt()]).then_inc(sem_c[0], 1))
            SYN.q.append(lambda E: E.wait_ge(sem_c[0], 1))
            _bail(VEC, v_ffn1)
            return nc
        v_xn2, v_xnk2 = layer_norm("b", VEC, v_ffn1, xready_k=mg1, ten_guard=ty1)
        if upto == "ln2":
            _bail(VEC, v_xn2)
            return nc

        with ExitStack() as actx:
            asb = lambda name, shape, dty: actx.enter_context(nc.sbuf_tensor(name, list(shape), dty))
            qTh2 = [asb(f"qTh{i}", [128, T], BF16) for i in range(2)]
            kTh2 = [asb(f"kTh{i}", [128, T], BF16) for i in range(2)]
            vtok2 = [asb(f"vtok{i}", [128, T], BF16) for i in range(2)]
            vT_scr = asb("vT_scr", [128, T], BF16)
            scb = asb("scb", [128, 128], BF16)
            qt_scr = asb("qt_scr", [128, 128], BF16)
            kt_scr = asb("kt_scr", [128, 128], BF16)
            Sb = asb("Sb", [128, 128], BF16)
            Sf = [asb(f"Sf{i}", [128, 128], F32) for i in range(2)]
            hb = (qTh2, kTh2, vtok2, vT_scr)
            sbufs = (scb, qt_scr, kt_scr, Sb, Sf)

            with ExitStack() as h1ctx:
                o1 = h1ctx.enter_context(nc.sbuf_tensor("o1", [128, HHALF * T], BF16))
                qh1 = h1ctx.enter_context(nc.sbuf_tensor("qh1", [128, HHALF * T], BF16))
                se1 = attn_half(0, v_xn2, v_xnk2, o1, qh1, hb, sbufs, 0)
                if upto == "attn1":
                    _bail(SCA, se1["s"])
                    return nc
                v_g1, _unused = attn_global(0, o1, qh1, se1)
                if upto == "attng1":
                    _bail(VEC, v_g1)
                    return nc
            with ExitStack() as h2ctx:
                o2 = h2ctx.enter_context(nc.sbuf_tensor("o2", [128, HHALF * T], BF16))
                qh2 = h2ctx.enter_context(nc.sbuf_tensor("qh2", [128, HHALF * T], BF16))
                se2 = attn_half(1, v_xn2, v_xnk2, o2, qh2, hb, sbufs, v_g1)
                if upto == "attn2":
                    _bail(SCA, se2["s"])
                    return nc
                v_g2, (mg2, tpo2) = attn_global(1, o2, qh2, se2)
                if upto == "attng2":
                    _bail(VEC, v_g2)
                    return nc

        v_xn3, v_xnk3 = layer_norm("c", VEC, v_g2, xready_k=mg2, ten_guard=tpo2)
        if upto == "ln3":
            _bail(VEC, v_xn3)
            return nc
        ffn_pass(1, v_xnk3)

        SYN.q.append(lambda E, s=SYN, v=SYN.n: E.wait_ge(s.sem, v))
        flush()

    return nc


_CACHE = {}


def _host_prep(L0_attn_x, L0_attn_Wqkv, L0_attn_bqkv, L0_attn_Wout, L0_attn_bout,
               L0_attn_slopes, L0_ffn_w_gate, L0_ffn_w_up, L0_ffn_w_down,
               L0_ffn_s_gate, L0_ffn_s_up, L0_ffn_s_down):
    f32 = np.float32
    x = np.asarray(L0_attn_x, f32)
    Wqkv = np.asarray(L0_attn_Wqkv, f32)
    Wout = np.asarray(L0_attn_Wout, f32)
    wg = np.asarray(L0_ffn_w_gate, np.int8)
    wu = np.asarray(L0_ffn_w_up, np.int8)
    wd = np.asarray(L0_ffn_w_down, np.int8)

    wqkvt = np.ascontiguousarray(
        Wqkv.reshape(16, 128, 48, 128).transpose(2, 1, 0, 3).reshape(48, 128, 2048))
    wgt = np.ascontiguousarray(
        wg.reshape(16, 128, 64, 128).transpose(2, 1, 0, 3).reshape(64, 128, 2048))
    wut = np.ascontiguousarray(
        wu.reshape(16, 128, 64, 128).transpose(2, 1, 0, 3).reshape(64, 128, 2048))
    wdt = np.ascontiguousarray(
        wd.reshape(4, 16, 128, 16, 128).transpose(0, 3, 2, 1, 4).reshape(4, 16, 128, 2048))
    woutt = np.ascontiguousarray(
        Wout.reshape(2, 8, 128, 16, 128).transpose(0, 3, 2, 1, 4).reshape(2, 16, 128, 1024))

    jj, ii = np.mgrid[0:128, 0:128]
    d0c = (-np.maximum(ii - jj, 0)).astype(f32)
    m01c = (ii >= jj).astype(f32)
    dqc = np.broadcast_to(-np.arange(128, dtype=f32)[None, :], (128, 128)).copy()
    dk16c = np.broadcast_to(-(128.0 - np.arange(128, dtype=f32))[:, None], (128, 16)).copy()
    dbd8c = np.broadcast_to(np.repeat(-128.0 * np.arange(8, dtype=f32), 16)[None, :], (128, 128)).copy()

    shared = dict(
        wqkvt=wqkvt, wgt=wgt, wut=wut, wdt=wdt, woutt=woutt,
        bqkv=np.asarray(L0_attn_bqkv, f32), sgate=np.asarray(L0_ffn_s_gate, f32),
        sup=np.asarray(L0_ffn_s_up, f32), sdown=np.asarray(L0_ffn_s_down, f32),
        boutv=np.asarray(L0_attn_bout, f32), slopes=np.asarray(L0_attn_slopes, f32),
        ident8=np.eye(128, dtype=np.int8), identf=np.eye(128, dtype=f32),
        d0c=d0c, m01c=m01c, dqc=dqc, dk16c=dk16c, dbd8c=dbd8c,
    )
    in_maps = []
    for c in range(NCORE):
        row = np.zeros(128, dtype=f32)
        zrow = np.zeros(128, dtype=f32)
        for r in range(8):
            if r < c:
                row[r * 16:(r + 1) * 16] = -1024.0 * (c - 1 - r)
                zrow[r * 16:(r + 1) * 16] = 1.0
        m = dict(shared)
        m["xin"] = np.ascontiguousarray(x[0, c * 1024:(c + 1) * 1024, :].T)
        m["dcrwc"] = np.broadcast_to(row[None, :], (128, 128)).copy()
        m["zcc"] = np.broadcast_to(zrow[None, :], (128, 128)).copy()
        in_maps.append(m)
    return in_maps


def kernel(**inputs):
    in_maps = _host_prep(**inputs)
    if "nc" not in _CACHE:
        _CACHE["nc"] = build_program()
    nc = _CACHE["nc"]
    res = run_bass_kernel_spmd(nc, in_maps, core_ids=list(range(NCORE)))
    out = np.empty((1, 8192, 2048), dtype=np.float32)
    for c in range(NCORE):
        out[0, c * 1024:(c + 1) * 1024, :] = np.asarray(res.results[c]["xout"]).T
    return out
